# version 14
# baseline (speedup 1.0000x reference)
import numpy as np
import contextlib
import concourse.bass as bass
import concourse.mybir as mybir
from concourse.bass_utils import run_bass_kernel_spmd

F32 = mybir.dt.float32
BF16 = mybir.dt.bfloat16
I32 = mybir.dt.int32
AF = mybir.ActivationFunctionType
ALU = mybir.AluOpType
AX = mybir.AxisListType

S, D, HID, E3 = 4096, 1024, 3072, 3072
NSEQ, NT, NG1, NB, NG3 = 2, 32, 8, 8, 4
EPS = 1e-5
ISQ = 1.0 / 32.0

ENGINES = ("tensor", "scalar", "vector", "gpsimd", "sync")
SEMS = ("pe", "act", "dve", "pool",
        "lda", "ldb", "ldc", "ldd",
        "wla", "wlb", "qa", "qb", "ka", "kb", "xba", "xbb",
        "sxa", "sxb", "qt", "ks", "misc", "psa", "psb", "qtr", "ksr")


class Prog:
    def __init__(self):
        self.ops = {e: [] for e in ENGINES}
        self.cnt = {s: 0 for s in SEMS}
        self.hw = {e: {s: 0 for s in SEMS} for e in ENGINES}

    def em(self, eng, fn, waits=(), inc=None, amt=1):
        w = []
        for item in waits:
            if item is None:
                continue
            sn, v = item
            if v is None or v <= 0 or self.hw[eng][sn] >= v:
                continue
            self.hw[eng][sn] = v
            w.append((sn, v))
        tok = None
        incs = []
        if inc is not None:
            self.cnt[inc] += amt
            tok = self.cnt[inc]
            incs.append((inc, amt))
        self.ops[eng].append((w, fn, incs))
        return tok

    FENCE_SEMS = ("pe", "act", "dve", "pool", "sxa", "sxb", "qt", "ks",
                  "psa", "psb")

    def fence(self):
        snap = [(sn, self.cnt[sn]) for sn in self.FENCE_SEMS]
        for e in ENGINES:
            self.em(e, None, waits=snap)


def build_nc(dbg=False):
    nc = bass.Bass("TRN2", target_bir_lowering=False)
    P = Prog()
    pe, act, dve, pool, syn = "tensor", "scalar", "vector", "gpsimd", "sync"

    t_in = nc.dram_tensor("t", [NSEQ, S, D], F32, kind="ExternalInput").ap()
    wqkv = nc.dram_tensor("wqkv", [E3, D], F32, kind="ExternalInput").ap()
    bqkv = nc.dram_tensor("bqkv", [E3], F32, kind="ExternalInput").ap()
    wg_i = nc.dram_tensor("wg", [HID, D], I32, kind="ExternalInput").ap()
    wu_i = nc.dram_tensor("wu", [HID, D], I32, kind="ExternalInput").ap()
    wd_i = nc.dram_tensor("wd", [D, HID], I32, kind="ExternalInput").ap()
    out = nc.dram_tensor("out", [NSEQ, S, D], F32, kind="ExternalOutput").ap()

    wqkvT = nc.dram_tensor("wqkvT", [24, 8, 128, 128], BF16, kind="Internal").ap()
    wgT = nc.dram_tensor("wgT", [24, 8, 128, 128], BF16, kind="Internal").ap()
    wuT = nc.dram_tensor("wuT", [24, 8, 128, 128], BF16, kind="Internal").ap()
    wdT = nc.dram_tensor("wdT", [24, 128, 1024], BF16, kind="Internal").ap()
    scratch_kind = "ExternalOutput" if dbg else "Internal"
    qscr = [nc.dram_tensor(f"qscr{i}", [NB, 8, 128, 512], BF16, kind=scratch_kind).ap()
            for i in range(NSEQ)]
    kscr = [nc.dram_tensor(f"kscr{i}", [NT, 8, 128, 128], BF16, kind=scratch_kind).ap()
            for i in range(NSEQ)]
    xnew = [nc.dram_tensor(f"xnew{i}", [NT, 128, 1024], F32, kind=scratch_kind).ap()
            for i in range(NSEQ)]
    if dbg:
        d_tiny = nc.dram_tensor("d_tiny", [1, 8], F32, kind="ExternalOutput").ap()
        d_lnxt = nc.dram_tensor("d_lnxt", [128, 4096], BF16, kind="ExternalOutput").ap()
        d_probs = nc.dram_tensor("d_probs", [128, 16384], BF16, kind="ExternalOutput").ap()
        d_recip = nc.dram_tensor("d_recip", [1, 512], F32, kind="ExternalOutput").ap()
        d_scale = nc.dram_tensor("d_scale", [128, 4], F32, kind="ExternalOutput").ap()
        d_x0 = nc.dram_tensor("d_x0", [128, 2048], F32, kind="ExternalOutput").ap()
        d_stats = nc.dram_tensor("d_stats", [128, 64], F32, kind="ExternalOutput").ap()
        d_ln0 = nc.dram_tensor("d_ln0", [128, 2048], BF16, kind="ExternalOutput").ap()

    es = contextlib.ExitStack()
    sb = lambda n, sh, dt: es.enter_context(nc.sbuf_tensor(n, sh, dt))

    varena = sb("varena", [128, 32 * 1024], BF16)
    parena = sb("parena", [128, 32 * 512], BF16)
    qarena = sb("qarena", [128, 2 * 4096], BF16)
    kbufs = sb("kbufs", [128, 8 * 1024], BF16)
    wbufs = sb("wbufs", [128, 2 * 4096], BF16)
    xa = sb("xa", [128, 4 * 1024], F32)
    xb = sb("xb", [128, 2 * 1024], F32)
    lnout = sb("lnout", [128, 4 * 1024], BF16)
    evac = sb("evac", [128, 2 * 1024], F32)
    dn2 = sb("dn2", [128, 1024], F32)
    silu_sb = sb("silu", [128, 1024], BF16)
    q2s = sb("q2s", [128, 8 * 512], BF16)
    stats = sb("stats", [128, 64], F32)
    ident = sb("ident", [128, 128], BF16)
    ident1 = sb("ident1", [1, 1], F32)
    ones_col = sb("ones_col", [128, 1], BF16)
    ones_row = sb("ones_row", [1, 512], BF16)
    ones_rf = sb("ones_rf", [1, 128], F32)
    epsv = sb("epsv", [128, 1], F32)
    bias_exp = sb("bias_exp", [128, 1], F32)
    tiny = sb("tiny", [1, 8], F32)
    recip_row = sb("recip_row", [1, 512], F32)

    bqT = sb("bqT", [128, 16], F32)
    bvrep = sb("bvrep", [128, 1024], F32)

    def st_sx(r, i): return stats[:, r * 24 + 6 * i: r * 24 + 6 * i + 6]
    def st_sext(r):  return stats[:, r * 24: r * 24 + 12]
    def st_mv(r):    return stats[:, r * 24 + 12: r * 24 + 14]
    def st_mean(r):  return stats[:, r * 24 + 12: r * 24 + 13]
    def st_var(r):   return stats[:, r * 24 + 13: r * 24 + 14]
    def st_lnv(r):   return stats[:, r * 24 + 14: r * 24 + 15]
    def st_rstd(r):  return stats[:, r * 24 + 15: r * 24 + 16]
    def st_negmr(r): return stats[:, r * 24 + 16: r * 24 + 17]
    def st_scale(j): return stats[:, 48 + j: 49 + j]

    def ln_stats(r, src_f32, extra_waits=()):
        b1 = P.em(dve, lambda e, o=st_sx(r, 0), i=src_f32[:, 0:512]:
                  e.bn_stats(out=o, in_=i), waits=list(extra_waits), inc="dve")
        b2 = P.em(dve, lambda e, o=st_sx(r, 1), i=src_f32[:, 512:1024]:
                  e.bn_stats(out=o, in_=i), inc="dve")
        ag = P.em(dve, lambda e, o=st_mv(r), i=st_sext(r):
                  e.bn_aggr(out=o, in_=i), waits=[("dve", b2)], inc="dve")
        lnt = P.em(act, lambda e, o=st_lnv(r), i=st_var(r):
                   e.activation(out=o, in_=i, func=AF.Ln, bias=epsv[:]),
                   waits=[("dve", ag)], inc="act")
        rs = P.em(act, lambda e, o=st_rstd(r), i=st_lnv(r):
                  e.activation(out=o, in_=i, func=AF.Exp, scale=-0.5),
                  waits=[("act", lnt)], inc="act")
        return ag, rs

    def _init0(g):
        return g.memset(ident[:], 0.0)
    init0_tok = P.em(pool, _init0, inc="pool")

    def _init(g):
        g.affine_select(out=ident[:], in_=ident[:], compare_op=ALU.not_equal,
                        fill=1.0, base=0, pattern=[[-1, 128]], channel_multiplier=1)
        g.memset(ident1[:], 1.0)
        g.memset(ones_col[:], 1.0)
        g.memset(ones_row[:], 1.0)
        g.memset(ones_rf[:], 1.0)
        g.memset(epsv[:], EPS)
        return g.memset(tiny[:], 0.0)
    init_tok = P.em(pool, _init, waits=[("pool", init0_tok)], inc="pool")

    pps = contextlib.ExitStack()
    ptr = [pps.enter_context(nc.psum_tensor(f"ptr{i}", [128, 1024], BF16))
           for i in range(2)]
    vf32 = varena.bitcast(F32)
    sstg = qarena
    cast_tok, tr_tok, cp_tok, stq_tok, ld_tok = {}, {}, {}, {}, {}
    sub_i = [0]

    def prep_chunk(ch, src_ap, nsub, dst_ap, stg_view, scale, in_i32):
        slot = ch % 3
        lsrc = vf32[:, slot * 4096: slot * 4096 + nsub * 1024]
        if in_i32:
            lsrc = lsrc.bitcast(I32)
        lsem = ("wla", "wlb", "qa")[slot]
        lt = (lsem, P.em(syn, lambda e, o=lsrc.rearrange(
                  "p (c d) -> p c d", c=nsub), i=src_ap:
                  e.dma_start(out=o, in_=i),
                  waits=[("act", cast_tok.get(ch - 3)), ("pool", init_tok)],
                  inc=lsem, amt=16))
        ld_tok[ch] = lt
        stg = sstg[:, (ch % 2) * 4096: (ch % 2) * 4096 + nsub * 1024]
        ct = pt = None
        for si in range(nsub):
            gsub = sub_i[0]
            sub_i[0] += 1
            lsl = gsub % 2
            csrc = lnout[:, lsl * 1024:(lsl + 1) * 1024]
            ct = P.em(act, lambda e, o=csrc,
                      i=lsrc[:, si * 1024:(si + 1) * 1024], sc=scale:
                      e.activation(out=o, in_=i, func=AF.Copy, scale=sc),
                      waits=[lt, ("pe", tr_tok.get(gsub - 2))], inc="act")
            tt = None
            for c in range(8):
                tt = P.em(pe, lambda e, o=ptr[lsl][:, c * 128:(c + 1) * 128],
                          i=csrc[:, c * 128:(c + 1) * 128]:
                          e.transpose(out=o, in_=i, identity=ident[:]),
                          waits=[("act", ct), ("dve", cp_tok.get(gsub - 2)),
                                 ("pool", init_tok)],
                          inc="pe" if c == 7 else None)
            tr_tok[gsub] = tt
            pt = P.em(dve, lambda e, o=stg[:, si * 1024:(si + 1) * 1024],
                      i=ptr[lsl][:, :]: e.tensor_copy(out=o, in_=i),
                      waits=[("pe", tt), stq_tok.get(ch - 2)], inc="dve")
            cp_tok[gsub] = pt
        cast_tok[ch] = ct
        psem = ("psa", "psb")[ch % 2]
        stq_tok[ch] = (psem, P.em(
            pool, lambda e, o=dst_ap, i=stg_view(stg):
            e.dma_start(out=o, in_=i),
            waits=[("dve", pt)], inc=psem, amt=16))

    ch = 0
    for wsrc, wdst, i32 in ((wqkv, wqkvT, False), (wg_i, wgT, True),
                            (wu_i, wuT, True)):
        for q in range(6):
            dst = wdst[4 * q: 4 * q + 4].rearrange("q c d k -> d q c k")
            prep_chunk(ch, wsrc[q * 512:(q + 1) * 512, :].rearrange(
                "(c p) d -> p c d", p=128), 4, dst,
                lambda st: st[:, 0:4096].rearrange("d (q2 c k) -> d q2 c k",
                                                   q2=4, c=8),
                1.0 if not i32 else 1 / 127.0, i32)
            ch += 1
    for dt_i in range(8):
        dst = wdT[0:24, :, dt_i * 128:(dt_i + 1) * 128].rearrange(
            "c d k -> d c k")
        prep_chunk(ch, wd_i[dt_i * 128:(dt_i + 1) * 128, :].rearrange(
            "p (c d) -> p c d", c=3), 3, dst,
            lambda st: st[:, 0:3072].rearrange("d (c k) -> d c k", c=24),
            1 / 127.0, True)
        ch += 1

    pbv = [pps.enter_context(nc.psum_tensor(f"pbv{i}", [128, 512], F32))
           for i in range(2)]
    bstg = parena.bitcast(F32)[0:16, 0:128]
    bstg2 = parena.bitcast(F32)[0:1, 1024:2048]
    b16 = lnout[0:16, 0:128]
    b16v = lnout[0:1, 1024:2048]
    bl = ("misc", P.em(syn, lambda e: e.dma_start(
        out=bstg, in_=bqkv[0:2048].rearrange("(c p) -> c p", p=128)),
        inc="misc", amt=16))
    blv = ("misc", P.em(syn, lambda e: e.dma_start(
        out=bstg2, in_=bqkv[2048:3072].rearrange("(a e) -> a e", a=1)),
        inc="misc", amt=16))
    c1 = P.em(act, lambda e: e.activation(out=b16, in_=bstg, func=AF.Copy),
              waits=[blv, ("pe", tr_tok[sub_i[0] - 2]), ("pool", init_tok)],
              inc="act")
    c2 = P.em(act, lambda e: e.activation(out=b16v, in_=bstg2, func=AF.Copy),
              waits=[blv, ("pe", tr_tok[sub_i[0] - 1])], inc="act")
    bt = P.em(pe, lambda e: e.transpose(out=ptr[0][:, 0:16], in_=b16,
                                        identity=ident[0:16, 0:16]),
              waits=[("act", c1), ("dve", cp_tok[sub_i[0] - 2])], inc="pe")
    bm0 = P.em(pe, lambda e: e.matmul(out=pbv[0][:, :], lhsT=ones_row[0:1, 0:128],
                                      rhs=b16v[:, 0:512], start=True, stop=True),
               waits=[("act", c2)], inc="pe")
    bm1 = P.em(pe, lambda e: e.matmul(out=pbv[1][:, :], lhsT=ones_row[0:1, 0:128],
                                      rhs=b16v[:, 512:1024], start=True, stop=True),
               inc="pe")
    bq_tok = P.em(act, lambda e: e.activation(out=bqT[:], in_=ptr[0][:, 0:16].bitcast(F32)[:, 0:16] if False else ptr[0][:, 0:16], func=AF.Copy),
                  waits=[("pe", bt)], inc="act")
    P.em(act, lambda e: e.activation(out=bvrep[:, 0:512], in_=pbv[0][:, :],
                                     func=AF.Copy), waits=[("pe", bm1)])
    bq_tok = P.em(act, lambda e: e.activation(out=bvrep[:, 512:1024],
                                              in_=pbv[1][:, :], func=AF.Copy),
                  inc="act")

    pps.close()
    P.fence()

    ring_last = {}

    for s in range(NSEQ):
        bq_gate = None
        p1s = contextlib.ExitStack()
        trP = [p1s.enter_context(nc.psum_tensor(f"s{s}trP{i}", [128, 1024], BF16))
               for i in range(2)]
        pqk = [p1s.enter_context(nc.psum_tensor(f"s{s}pqk{i}", [128, 512], F32))
               for i in range(2)]
        pv = [p1s.enter_context(nc.psum_tensor(f"s{s}pv{i}", [128, 512], F32))
              for i in range(2)]
        pqn = p1s.enter_context(nc.psum_tensor(f"s{s}pqn", [1, 512], F32))
        pkn = p1s.enter_context(nc.psum_tensor(f"s{s}pkn", [1, 512], F32))

        LNxT = parena[:, 0: 8 * 512]
        QTst = parena[:, 8 * 512: 16 * 512]
        Kst = parena[:, 16 * 512: 24 * 512]

        apply_tok, transp_tok, copy_tok = {}, {}, {}
        dbg_gate = {}
        ev_tok, vql_tok = {}, {}
        wset_tok = {}
        qn_tok, kn_tok, qmax_tok, kmax_tok = {}, {}, {}, {}
        vlast_tok = {}
        qt_dma, kscr_dma = {}, {}
        qt_relay, ks_relay = {}, {}
        sq2_tok = {}

        def wload(g, si):
            gs = g * 6 + si
            par = gs % 2
            dst = wbufs[:, par * 4096:(par + 1) * 4096].rearrange(
                "d (ee c k) -> d ee c k", ee=4, c=8)
            src = wqkvT[4 * si: 4 * si + 4].rearrange("ee c d k -> d ee c k")
            if si < 2:
                free = ("pe", vlast_tok.get(g - 1))
            else:
                free = ("pe", P.cnt["pe"])
            wsem = "wla" if par == 0 else "wlb"
            wset_tok[gs] = (wsem, P.em(syn, lambda e, o=dst, i=src:
                                e.dma_start(out=o, in_=i),
                                waits=[free], inc=wsem, amt=16))

        def chain_p1_load(tile):
            sl4 = tile % 4
            xsrc = xa[:, sl4 * 1024:(sl4 + 1) * 1024]
            lsem = ("lda", "ldb", "ldc", "ldd")[sl4]
            load_tok[tile] = (lsem, P.em(
                syn, lambda e, o=xsrc, i=t_in[s, tile * 128:(tile + 1) * 128, :]:
                e.dma_start(out=o, in_=i),
                waits=[apply_tok.get(tile - 4), dbg_gate.get(tile - 2)],
                inc=lsem, amt=16))

        def chain_p1_ln(tile):
            sl4 = tile % 4
            r = tile % 2
            xsrc = xa[:, sl4 * 1024:(sl4 + 1) * 1024]
            lo = lnout[:, sl4 * 1024:(sl4 + 1) * 1024]
            rl = ring_last.get(r)
            extra = [load_tok[tile]] + ([("act", rl[0]), ("dve", rl[1])]
                                        if rl else [])
            ag, rs = ln_stats(r, xsrc, extra)
            ap_ = P.em(dve, lambda e, o=lo, i=xsrc, m=st_mean(r), rr=st_rstd(r):
                       e.tensor_scalar(out=o, in0=i, scalar1=m, scalar2=rr,
                                       op0=ALU.subtract, op1=ALU.mult),
                       waits=[("act", rs), ("dve", ag),
                              ("pe", transp_tok.get(tile - 4))],
                       inc="dve")
            apply_tok[tile] = ("dve", ap_)
            ring_last[r] = (rs, ap_)

        load_tok = {}
        for tile in range(4):
            chain_p1_load(tile)
        for tile in range(4):
            chain_p1_ln(tile)

        for g in range(NG1):
            wload(g, 0)
            wload(g, 1)

            for j in range(4):
                tile = 4 * g + j
                sl = tile % 2
                sl4 = tile % 4
                lo = lnout[:, sl4 * 1024:(sl4 + 1) * 1024]
                ap_ = apply_tok[tile]
                tt = None
                for c in range(8):
                    tt = P.em(pe, lambda e, o=trP[sl][:, c * 128:(c + 1) * 128],
                              i=lo[:, c * 128:(c + 1) * 128]:
                              e.transpose(out=o, in_=i, identity=ident[:]),
                              waits=[ap_, ("dve", copy_tok.get(tile - 2))],
                              inc="pe" if c == 7 else None)
                transp_tok[tile] = tt
                dstv = LNxT.rearrange("d (c tk) -> d c tk", c=8)[
                    :, :, j * 128:(j + 1) * 128]
                srcv = trP[sl][:, :].rearrange("d (c tk) -> d c tk", c=8)
                copy_tok[tile] = P.em(
                    dve, lambda e, o=dstv, i=srcv: e.tensor_copy(out=o, in_=i),
                    waits=[("pe", tt), ("pe", vlast_tok.get(g - 1))], inc="dve")
            lnx_ready = copy_tok[4 * g + 3]

            for e_ in range(16):
                gs = g * 6 + e_ // 4
                par = gs % 2
                wv = wbufs[:, par * 4096:(par + 1) * 4096]
                bank = e_ % 2
                mwaits = [("dve", lnx_ready), wset_tok[gs],
                          ("act", ev_tok.get((g, e_ - 2),
                                             ev_tok.get((g - 1, e_ + 14)))),
                          ("act", bq_tok if g == 0 else None)]
                mmt = None
                for c in range(8):
                    lhs = wv[:, (e_ % 4) * 1024 + c * 128:
                             (e_ % 4) * 1024 + (c + 1) * 128]
                    rhs = LNxT[:, c * 512:(c + 1) * 512]
                    mmt = P.em(pe, lambda e, o=pqk[bank][:, :], l=lhs, rr=rhs,
                               st=(c == 0), sp=(c == 7):
                               e.matmul(out=o, lhsT=l, rhs=rr, start=st, stop=sp),
                               waits=mwaits if c == 0 else (),
                               inc="pe" if c == 7 else None)
                if e_ < 8:
                    dst = QTst[:, e_ * 512:(e_ + 1) * 512]
                    stage_free = qt_dma.get(g - 1)
                else:
                    dst = Kst[:, (e_ - 8) * 512:(e_ - 8 + 1) * 512]
                    stage_free = kscr_dma.get(g - 1)
                et = P.em(act, lambda e, o=dst, i=pqk[bank][:, :],
                          b=bqT[:, e_: e_ + 1]:
                          e.activation(out=o, in_=i, func=AF.Identity, bias=b),
                          waits=[("pe", mmt), stage_free], inc="act")
                ev_tok[(g, e_)] = et
                slq = e_ % 8
                sq_w = [("act", et)]
                if e_ < 8:
                    sq_w.append(("pe", kn_tok.get(g - 1)))
                else:
                    sq_w.append(("pe", qn_tok.get(g)))
                sq2_tok[(g, e_)] = P.em(
                    dve, lambda e, o=q2s[:, slq * 512:(slq + 1) * 512], i=dst:
                    e.tensor_mul(out=o, in0=i, in1=i), waits=sq_w, inc="dve")

                if e_ % 4 == 3:
                    wload(g, e_ // 4 + 2)
                if e_ == 3 and g + 1 < NG1:
                    for tl in range(4 * (g + 1), 4 * (g + 1) + 4):
                        chain_p1_load(tl)
                if e_ in (8, 10, 12, 14) and g + 1 < NG1:
                    chain_p1_ln(4 * (g + 1) + (e_ - 8) // 2)
                if e_ == 7:
                    tok = None
                    for c in range(8):
                        tok = P.em(pe, lambda e, o=pqn[:, :],
                                   rr=q2s[:, c * 512:(c + 1) * 512],
                                   st=(c == 0), sp=(c == 7):
                                   e.matmul(out=o, lhsT=ones_col[:], rhs=rr,
                                            start=st, stop=sp),
                                   waits=[("dve", sq2_tok[(g, 7)]),
                                          ("dve", qmax_tok.get(g - 1))],
                                   inc="pe" if c == 7 else None)
                    qn_tok[g] = tok
                    rm = P.em(dve, lambda e, i=pqn[:, :]: e.reduce_max(
                        out=tiny[:, 5:6], in_=i, axis=AX.X),
                        waits=[("pe", tok)], inc="dve")
                    qmax_tok[g] = P.em(dve, lambda e: e.tensor_max(
                        out=tiny[:, 0:1], in0=tiny[:, 0:1], in1=tiny[:, 5:6]),
                        waits=[("dve", rm)], inc="dve")

            for j in range(4):
                tile = 4 * g + j
                for hf in range(2):
                    gs = g * 6 + 4 + hf
                    par = gs % 2
                    wv4 = wbufs[:, par * 4096:(par + 1) * 4096].rearrange(
                        "d (ee c k) -> d ee c k", ee=4, c=8)
                    mwaits = [("dve", lnx_ready), wset_tok[gs],
                              ("dve", vql_tok.get((g, j - 1, hf),
                                                  vql_tok.get((g - 1, 3, hf))))]
                    mmt = None
                    for c in range(8):
                        lhs = LNxT[:, c * 512 + j * 128: c * 512 + (j + 1) * 128]
                        mmt = P.em(pe, lambda e, o=pv[hf][:, :], l=lhs,
                                   rr=wv4[:, :, c, :], st=(c == 0), sp=(c == 7):
                                   e.matmul(out=o, lhsT=l, rhs=rr,
                                            start=st, stop=sp),
                                   waits=mwaits if c == 0 else (),
                                   inc="pe" if c == 7 else None)
                    dst = varena[:, tile * 1024 + hf * 512:
                                 tile * 1024 + (hf + 1) * 512]
                    vql_tok[(g, j, hf)] = P.em(
                        dve, lambda e, o=dst, i=pv[hf][:, :],
                        b=bvrep[:, hf * 512:(hf + 1) * 512]:
                        e.tensor_add(out=o, in0=i, in1=b),
                        waits=[("pe", mmt)], inc="dve")
                if j == 0:
                        tok = None
                        for c in range(8):
                            tok = P.em(pe, lambda e, o=pkn[:, :],
                                       rr=q2s[:, c * 512:(c + 1) * 512],
                                       st=(c == 0), sp=(c == 7):
                                       e.matmul(out=o, lhsT=ones_col[:], rhs=rr,
                                                start=st, stop=sp),
                                       waits=[("dve", sq2_tok[(g, 15)]),
                                              ("dve", kmax_tok.get(g - 1))],
                                       inc="pe" if c == 7 else None)
                        kn_tok[g] = tok
                        rm = P.em(dve, lambda e, i=pkn[:, :]: e.reduce_max(
                            out=tiny[:, 6:7], in_=i, axis=AX.X),
                            waits=[("pe", tok)], inc="dve")
                        kmax_tok[g] = P.em(dve, lambda e: e.tensor_max(
                            out=tiny[:, 1:2], in0=tiny[:, 1:2], in1=tiny[:, 6:7]),
                            waits=[("dve", rm)], inc="dve")

            vlast_tok[g] = P.cnt["pe"]

            qt_dma[g] = ("qt", P.em(pool, lambda e,
                             o=qscr[s][g].rearrange("c d q -> d c q"),
                             i=QTst.rearrange("d (c q) -> d c q", c=8):
                             e.dma_start(out=o, in_=i),
                             waits=[("act", ev_tok[(g, 7)])], inc="qt", amt=16))
            qt_relay[g] = P.em(pool, lambda e: e.nop(),
                               waits=[("qt", P.cnt["qt"])], inc="qtr")
            tok = None
            for ec in range(8):
                ksrc = Kst[:, ec * 512:(ec + 1) * 512].rearrange(
                    "d (tt k) -> d tt k", tt=4)
                kdst = kscr[s][4 * g: 4 * g + 4, ec].rearrange("tt d k -> d tt k")
                tok = P.em(pool, lambda e, o=kdst, i=ksrc:
                           e.dma_start(out=o, in_=i),
                           waits=[("act", ev_tok[(g, 15)])], inc="ks", amt=16)
            kscr_dma[g] = ("ks", tok)
            ks_relay[g] = P.em(pool, lambda e: e.nop(),
                               waits=[("ks", P.cnt["ks"])], inc="ksr")

        P.em(act, lambda e: e.activation(out=tiny[:, 2:3], in_=tiny[:, 0:1],
                                         func=AF.Ln),
             waits=[("dve", qmax_tok[NG1 - 1]), ("dve", kmax_tok[NG1 - 1])])
        l2 = P.em(act, lambda e: e.activation(out=tiny[:, 3:4], in_=tiny[:, 1:2],
                                              func=AF.Ln), inc="act")
        ad = P.em(dve, lambda e: e.tensor_add(out=tiny[:, 2:3], in0=tiny[:, 2:3],
                                              in1=tiny[:, 3:4]),
                  waits=[("act", l2)], inc="dve")
        bb = P.em(act, lambda e: e.activation(out=tiny[:, 4:5], in_=tiny[:, 2:3],
                                              func=AF.Exp, scale=0.5),
                  waits=[("dve", ad)], inc="act")
        nb_ = P.em(dve, lambda e: e.tensor_scalar(
            out=tiny[:, 4:5], in0=tiny[:, 4:5], scalar1=-ISQ, scalar2=None,
            op0=ALU.mult), waits=[("act", bb)], inc="dve")
        bm = P.em(pe, lambda e, o=pv[0][:, 0:1]: e.matmul(
            out=o, lhsT=ones_rf[:], rhs=tiny[:, 4:5], start=True, stop=True),
                  waits=[("dve", nb_), ("dve", vql_tok[(NG1 - 1, 3, 0)])],
                  inc="pe")
        be = P.em(act, lambda e, i=pv[0][:, 0:1]: e.activation(
            out=bias_exp[:], in_=i, func=AF.Copy),
                  waits=[("pe", bm)], inc="act")

        if dbg and s == 0:
            P.em(pool, lambda e: e.dma_start(out=d_tiny, in_=tiny[:]),
                 waits=[("act", be)], inc="misc", amt=16)
            P.em(pool, lambda e: e.dma_start(out=d_lnxt, in_=parena[:, 0:4096]),
                 waits=[("pe", vlast_tok[NG1 - 1])], inc="misc", amt=16)

        p1s.close()

        p2s = contextlib.ExitStack()
        psc = [p2s.enter_context(nc.psum_tensor(f"s{s}psc{i}", [128, 512], F32))
               for i in range(3)]
        pav = [p2s.enter_context(nc.psum_tensor(f"s{s}pav{i}", [128, 1024], F32))
               for i in range(2)]
        psb7 = p2s.enter_context(nc.psum_tensor(f"s{s}psb7", [128, 512], F32))

        exp_tok, sc_tok = {}, {}
        rc_last = {}
        probs_free = {}
        q_tok, q_free = {}, {}
        kset_tok, kset_free = {}, {}
        evac_tok, add_tok, xn_dma = {}, {}, {}
        scale_cp_tok = {}

        def emit_av_phase(B):
            stok = None
            for ti in range(32):
                stok = P.em(pe, lambda e, o=psb7[0:1, :],
                            rr=parena[:, ti * 512:(ti + 1) * 512],
                            st=(ti == 0), sp=(ti == 31):
                            e.matmul(out=o, lhsT=ones_col[:], rhs=rr,
                                     start=st, stop=sp),
                            waits=([("act", exp_tok[(B, 31)]),
                                    ("act", scale_cp_tok.get(B - 1))]
                                   if ti == 0 else ()),
                            inc="pe" if ti == 31 else None)
            rc = P.em(dve, lambda e, i=psb7[0:1, :]: e.reciprocal(
                out=recip_row[:], in_=i),
                      waits=[("pe", stok)], inc="dve")
            rc_last["rc"] = rc
            ttok = None
            for j in range(4):
                ttok = P.em(pe, lambda e, o=psb7[:, 4 + j: 5 + j],
                            i=recip_row[0:1, j * 128:(j + 1) * 128]:
                            e.transpose(out=o, in_=i, identity=ident1[:]),
                            waits=[("dve", rc)], inc="pe" if j == 3 else None)
            sct = None
            for j in range(4):
                sct = P.em(act, lambda e, o=st_scale(j), i=psb7[:, 4 + j: 5 + j]:
                           e.activation(out=o, in_=i, func=AF.Copy),
                           waits=[("pe", ttok)], inc="act")
            scale_cp_tok[B] = sct
            for j in range(4):
                pr = j % 2
                tile = 4 * B + j
                r = j % 2
                w0 = [("act", exp_tok[(B, 31)]),
                      ("act", evac_tok.get((B, j - 2),
                                           evac_tok.get((B - 1, 2 + (j % 2)))))]
                av1 = None
                for hf in range(2):
                    for ti in range(32):
                        lhs = parena[:, ti * 512 + j * 128:
                                     ti * 512 + (j + 1) * 128]
                        rhs = varena[:, ti * 1024 + hf * 512:
                                     ti * 1024 + (hf + 1) * 512]
                        av1 = P.em(pe, lambda e,
                                   o=pav[pr][:, hf * 512:(hf + 1) * 512],
                                   l=lhs, rr=rhs, st=(ti == 0), sp=(ti == 31):
                                   e.matmul(out=o, lhsT=l, rhs=rr,
                                            start=st, stop=sp),
                                   waits=w0 if (ti == 0 and hf == 0) else (),
                                   inc="pe" if ti == 31 else None)
                xsl = xb[:, (j % 2) * 1024:((j % 2) + 1) * 1024]
                xsem = "xba" if j % 2 == 0 else "xbb"
                xl = (xsem, P.em(syn, lambda e, o=xsl,
                          i=t_in[s, tile * 128:(tile + 1) * 128, :]:
                          e.dma_start(out=o, in_=i),
                          waits=[("dve", add_tok.get((B, j - 2),
                                                     add_tok.get((B - 1, 2 + (j % 2)))))],
                          inc=xsem, amt=16))
                esl = evac[:, (j % 2) * 1024:((j % 2) + 1) * 1024]
                et = P.em(act, lambda e, o=esl, i=pav[pr][:, :], sc=st_scale(j):
                          e.activation(out=o, in_=i, func=AF.Copy, scale=sc),
                          waits=[("pe", av1), ("act", sct),
                                 xn_dma.get((B, j - 2),
                                            xn_dma.get((B - 1, 2 + (j % 2))))],
                          inc="act")
                evac_tok[(B, j)] = et
                rl = ring_last.get(r)
                extra = [("act", et)] + ([("act", rl[0]), ("dve", rl[1])]
                                         if rl else [])
                ag, rs = ln_stats(r, esl, extra)
                ts = P.em(dve, lambda e, o=dn2[:], i=esl, m=st_mean(r),
                          rr=st_rstd(r):
                          e.tensor_scalar(out=o, in0=i, scalar1=m, scalar2=rr,
                                          op0=ALU.subtract, op1=ALU.mult),
                          waits=[("act", rs), ("dve", ag)], inc="dve")
                ring_last[r] = (rs, ts)
                at = P.em(dve, lambda e, o=esl, i=dn2[:], x=xsl:
                          e.tensor_add(out=o, in0=i, in1=x),
                          waits=[("dve", ts), xl], inc="dve")
                add_tok[(B, j)] = at
                ssem = "sxa" if j % 2 == 0 else "sxb"
                xn_dma[(B, j)] = (ssem, P.em(pool, lambda e, o=xnew[s][tile], i=esl:
                                      e.dma_start(out=o, in_=i),
                                      waits=[("dve", at)], inc=ssem, amt=16))
                if j == 3:
                    probs_free[B] = av1

        for b in range(NB):
            qdst = qarena[:, (b % 2) * 4096:((b % 2) + 1) * 4096].rearrange(
                "d (c q) -> d c q", c=8)
            qsem = "qa" if b % 2 == 0 else "qb"
            q_tok[b] = (qsem, P.em(syn, lambda e, o=qdst,
                            i=qscr[s][b].rearrange("c d q -> d c q"):
                            e.dma_start(out=o, in_=i),
                            waits=[("pe", q_free.get(b - 2)),
                                   ("qtr", qt_relay[b])],
                            inc=qsem, amt=16))
            if b > 0:
                emit_av_phase(b - 1)
            for ti in range(32):
                if ti % 4 == 0:
                    si = ti // 4
                    kdst = kbufs[:, (si % 2) * 4096:((si % 2) + 1) * 4096].rearrange(
                        "d (tt c k) -> d tt c k", tt=4, c=8)
                    ksrc = kscr[s][4 * si: 4 * si + 4].rearrange(
                        "tt c d k -> d tt c k")
                    ksem = "ka" if si % 2 == 0 else "kb"
                    kset_tok[(b, si)] = (ksem, P.em(
                        syn, lambda e, o=kdst, i=ksrc: e.dma_start(out=o, in_=i),
                        waits=[("pe", kset_free.get((b, si - 2),
                                                    kset_free.get((b - 1, 6 + si)))),
                               ("ksr", ks_relay[si])],
                        inc=ksem, amt=16))
                bank = ti % 3
                if ti >= 3:
                    prev_exp = exp_tok[(b, ti - 3)]
                else:
                    prev_exp = exp_tok.get((b - 1, {0: 30, 1: 31, 2: 29}[ti]))
                w0 = [kset_tok[(b, ti // 4)], q_tok[b],
                      ("act", prev_exp), ("act", be),
                      ("qtr", qt_relay[NG1 - 1]), ("ksr", ks_relay[NG1 - 1])]
                mm = None
                for c in range(8):
                    lhs = kbufs[:, (((ti // 4) % 2) * 4 + ti % 4) * 1024 + c * 128:
                                (((ti // 4) % 2) * 4 + ti % 4) * 1024 + (c + 1) * 128]
                    rhs = qarena[:, (b % 2) * 4096 + c * 512:
                                 (b % 2) * 4096 + (c + 1) * 512]
                    mm = P.em(pe, lambda e, o=psc[bank][:, :], l=lhs, rr=rhs,
                              st=(c == 0), sp=(c == 7):
                              e.matmul(out=o, lhsT=l, rhs=rr, start=st, stop=sp),
                              waits=w0 if c == 0 else (),
                              inc="pe" if c == 7 else None)
                sc_tok[(b, ti)] = mm
                if ti % 4 == 3:
                    kset_free[(b, ti // 4)] = mm
                exp_tok[(b, ti)] = P.em(
                    act, lambda e, o=parena[:, ti * 512:(ti + 1) * 512],
                    i=psc[bank][:, :]:
                    e.activation(out=o, in_=i, func=AF.Exp, scale=ISQ,
                                 bias=bias_exp[:]),
                    waits=[("pe", mm)], inc="act")
            q_free[b] = sc_tok[(b, 31)]
        emit_av_phase(NB - 1)

        if dbg and s == 0:
            P.em(pool, lambda e: e.dma_start(out=d_probs, in_=parena[:, :]),
                 waits=[("act", exp_tok[(NB - 1, 31)])], inc="misc", amt=16)
            P.em(pool, lambda e: e.dma_start(out=d_recip, in_=recip_row[:]),
                 waits=[("dve", rc_last["rc"])], inc="misc", amt=16)
            P.em(pool, lambda e: e.dma_start(out=d_scale, in_=stats[:, 48:52]),
                 waits=[("act", scale_cp_tok[NB - 1])], inc="misc", amt=16)

        p2s.close()
        p2_act_done = evac_tok[(NB - 1, 3)]
        p2_av_done = probs_free[NB - 1]

        p3s = contextlib.ExitStack()
        pg = p3s.enter_context(nc.psum_tensor(f"s{s}pg", [128, 1024], F32))
        pu = p3s.enter_context(nc.psum_tensor(f"s{s}pu", [128, 1024], F32))
        pd = [p3s.enter_context(nc.psum_tensor(f"s{s}pd{i}", [128, 1024], F32))
              for i in range(2)]
        trv = pg[:, :].bitcast(BF16)
        LNx2T = varena[:, 0: 8 * 1024]

        def hact_ap(h):
            if h < 16:
                return parena[:, h * 1024:(h + 1) * 1024]
            return qarena[:, (h - 16) * 1024:(h - 16 + 1) * 1024]

        apply3_tok, transp3_tok, copy3_tok = {}, {}, {}
        silu_tok, hact_tok, gate_tok, up_tok = {}, {}, {}, {}
        wpair_tok, wpair_free = {}, {}
        dn_ev_tok, add3_tok, out_dma = {}, {}, {}
        gu_last, dn_last = {}, {}

        load3_tok = {}

        def chain_p3_load(g, j):
            tile = 8 * g + j
            sl4 = j % 4
            xsrc = xa[:, sl4 * 1024:(sl4 + 1) * 1024]
            lsem = ("lda", "ldb", "ldc", "ldd")[sl4]
            xw = [apply3_tok.get((g, j - 4), apply3_tok.get((g - 1, j + 4))),
                  xn_dma[(tile // 4, tile % 4)]]
            if g == 0 and j < 4:
                xw.append(apply_tok.get(28 + j))
            load3_tok[(g, j)] = (lsem, P.em(
                syn, lambda e, o=xsrc, i=xnew[s][tile]:
                e.dma_start(out=o, in_=i), waits=xw, inc=lsem, amt=16))

        def chain_p3_ln(g, j):
            sl4 = j % 4
            r = j % 2
            xsrc = xa[:, sl4 * 1024:(sl4 + 1) * 1024]
            lo = lnout[:, sl4 * 1024:(sl4 + 1) * 1024]
            rl = ring_last.get(r)
            extra = [load3_tok[(g, j)]] + ([("act", rl[0]), ("dve", rl[1])]
                                           if rl else [])
            ag, rs = ln_stats(r, xsrc, extra)
            aw = [("act", rs), ("dve", ag),
                  ("pe", transp3_tok.get((g, j - 4),
                                         transp3_tok.get((g - 1, j + 4))))]
            if g == 0 and j < 4:
                aw.append(("pe", transp_tok[28 + j]))
            ap_ = P.em(dve, lambda e, o=lo, i=xsrc, m=st_mean(r), rr=st_rstd(r):
                       e.tensor_scalar(out=o, in0=i, scalar1=m, scalar2=rr,
                                       op0=ALU.subtract, op1=ALU.mult),
                       waits=aw, inc="dve")
            apply3_tok[(g, j)] = ("dve", ap_)
            ring_last[r] = (rs, ap_)

        for jj in range(4):
            chain_p3_load(0, jj)
        wd_tok = None
        for q4 in range(6):
            ddst = varena[:, (8 + 4 * q4) * 1024:(8 + 4 * q4 + 4) * 1024].rearrange(
                "d (c k) -> d c k", c=4)
            wd_tok = ("misc", P.em(syn, lambda e, o=ddst,
                          i=wdT[4 * q4: 4 * q4 + 4].rearrange("c d k -> d c k"):
                          e.dma_start(out=o, in_=i),
                          waits=[("pe", p2_av_done)], inc="misc", amt=16))


        def p3_transp(g, j):
                sl4 = j % 4
                lo = lnout[:, sl4 * 1024:(sl4 + 1) * 1024]
                ap_ = apply3_tok[(g, j)]
                sl = j % 2
                tt = None
                for c in range(8):
                    tt = P.em(pe, lambda e,
                              o=trv[:, sl * 1024 + c * 128: sl * 1024 + (c + 1) * 128],
                              i=lo[:, c * 128:(c + 1) * 128]:
                              e.transpose(out=o, in_=i, identity=ident[:]),
                              waits=[ap_, ("dve", copy3_tok.get((g, j - 2))),
                                     ("pe", gu_last.get(g - 1)),
                                     ("act", silu_tok.get((g - 1, 23))),
                                     ("act", exp_tok[(NB - 1, 31)]
                                      if g == 0 else None)],
                              inc="pe" if c == 7 else None)
                transp3_tok[(g, j)] = tt
                dstv = LNx2T.rearrange("d (c tk) -> d c tk", c=8)[
                    :, :, j * 128:(j + 1) * 128]
                srcv = trv[:, sl * 1024:(sl + 1) * 1024].rearrange(
                    "d (c tk) -> d c tk", c=8)
                copy3_tok[(g, j)] = P.em(
                    dve, lambda e, o=dstv, i=srcv: e.tensor_copy(out=o, in_=i),
                    waits=[("pe", tt), ("pe", gu_last.get(g - 1))], inc="dve")

        for g in range(NG3):
            for j in range(4):
                if (g, j) not in load3_tok:
                    chain_p3_load(g, j)
            for j in range(4):
                if (g, j) not in apply3_tok:
                    chain_p3_ln(g, j)
            for j in range(4, 8):
                if (g, j) not in load3_tok:
                    chain_p3_load(g, j)
            for j in range(4):
                p3_transp(g, j)
            for j in range(4, 8):
                chain_p3_ln(g, j)
            for j in range(4, 8):
                p3_transp(g, j)
            lnx2_ready = copy3_tok[(g, 7)]

            for h in range(24):
                pi = h // 2
                par = pi % 2
                hh = h % 2
                if hh == 0:
                    wsem = "wla" if par == 0 else "wlb"
                    for wi, (wt, off) in enumerate([(wgT, 0), (wuT, 2048)]):
                        ddst = wbufs[:, par * 4096 + off: par * 4096 + off + 2048]
                        ddst = ddst.rearrange("d (hc c k) -> d hc c k", hc=2, c=8)
                        tok = P.em(
                            syn, lambda e, o=ddst,
                            i=wt[h:h + 2].rearrange("hc c d k -> d hc c k"):
                            e.dma_start(out=o, in_=i),
                            waits=[("pe", wpair_free.get((g, pi - 2),
                                                         wpair_free.get((g - 1, 10 + pi)))),
                                   ("pe", vlast_tok[NG1 - 1] if g == 0 else None)],
                            inc=wsem, amt=16)
                    wpair_tok[(g, pi)] = (wsem, tok)
                gw = [("dve", lnx2_ready), wpair_tok[(g, pi)],
                      ("act", silu_tok.get((g, h - 1), silu_tok.get((g - 1, 23))))]
                mm = None
                for hf in range(2):
                    for c in range(8):
                        lhs = wbufs[:, par * 4096 + hh * 1024 + c * 128:
                                    par * 4096 + hh * 1024 + (c + 1) * 128]
                        rhs = LNx2T[:, c * 1024 + hf * 512: c * 1024 + (hf + 1) * 512]
                        mm = P.em(pe, lambda e, o=pg[:, hf * 512:(hf + 1) * 512],
                                  l=lhs, rr=rhs, st=(c == 0), sp=(c == 7):
                                  e.matmul(out=o, lhsT=l, rhs=rr, start=st, stop=sp),
                                  waits=gw if (c == 0 and hf == 0) else (),
                                  inc="pe" if (c == 7 and hf == 1) else None)
                gate_tok[(g, h)] = mm
                uw = [wpair_tok[(g, pi)],
                      ("dve", hact_tok.get((g, h - 1), hact_tok.get((g - 1, 23)))),
                      ("act", p2_act_done if (g == 0 and h == 0) else None)]
                for hf in range(2):
                    for c in range(8):
                        lhs = wbufs[:, par * 4096 + 2048 + hh * 1024 + c * 128:
                                    par * 4096 + 2048 + hh * 1024 + (c + 1) * 128]
                        rhs = LNx2T[:, c * 1024 + hf * 512: c * 1024 + (hf + 1) * 512]
                        mm = P.em(pe, lambda e, o=pu[:, hf * 512:(hf + 1) * 512],
                                  l=lhs, rr=rhs, st=(c == 0), sp=(c == 7):
                                  e.matmul(out=o, lhsT=l, rhs=rr, start=st, stop=sp),
                                  waits=uw if (c == 0 and hf == 0) else (),
                                  inc="pe" if (c == 7 and hf == 1) else None)
                up_tok[(g, h)] = mm
                if hh == 1:
                    wpair_free[(g, pi)] = mm
                if h == 16 and g + 1 < NG3:
                    for jn in range(4):
                        chain_p3_load(g + 1, jn)
                if h == 23:
                    gu_last[g] = mm
                st_ = P.em(act, lambda e, o=silu_sb[:], i=pg[:, :]:
                           e.activation(out=o, in_=i, func=AF.Silu),
                           waits=[("pe", gate_tok[(g, h)]),
                                  ("dve", hact_tok.get((g, h - 1),
                                                       hact_tok.get((g - 1, 23))))],
                           inc="act")
                silu_tok[(g, h)] = st_
                hact_tok[(g, h)] = P.em(
                    dve, lambda e, o=hact_ap(h), i=silu_sb[:], u=pu[:, :]:
                    e.tensor_mul(out=o, in0=i, in1=u),
                    waits=[("act", st_), ("pe", up_tok[(g, h)]),
                           ("pe", dn_last.get(g - 1))],
                    inc="dve")
                if h >= 20 and g + 1 < NG3:
                    chain_p3_ln(g + 1, h - 20)
            hact_ready = hact_tok[(g, 23)]

            for j in range(8):
                tile = 8 * g + j
                pr = j % 2
                r = j % 2
                xsl = xb[:, (j % 2) * 1024:((j % 2) + 1) * 1024]
                xsem = "xba" if j % 2 == 0 else "xbb"
                xl = (xsem, P.em(syn, lambda e, o=xsl, i=xnew[s][tile]:
                          e.dma_start(out=o, in_=i),
                          waits=[("dve", add3_tok.get((g, j - 2),
                                                      add3_tok.get((g - 1, 6 + (j % 2))))),
                                 xn_dma[(tile // 4, tile % 4)]],
                          inc=xsem, amt=16))
                w0 = [("dve", hact_ready), wd_tok,
                      ("act", dn_ev_tok.get((g, j - 2),
                                            dn_ev_tok.get((g - 1, 6 + (j % 2))))),
                      ("act", p2_act_done if (g == 0 and j < 2) else None),
                      ("act", scale_cp_tok[NB - 1] if (g == 0 and j < 2) else None)]
                mm = None
                for hf in range(2):
                    for h in range(24):
                        lhs = hact_ap(h)[:, j * 128:(j + 1) * 128]
                        rhs = varena[:, (8 + h) * 1024 + hf * 512:
                                     (8 + h) * 1024 + (hf + 1) * 512]
                        mm = P.em(pe, lambda e,
                                  o=pd[pr][:, hf * 512:(hf + 1) * 512],
                                  l=lhs, rr=rhs, st=(h == 0), sp=(h == 23):
                                  e.matmul(out=o, lhsT=l, rhs=rr,
                                           start=st, stop=sp),
                                  waits=w0 if (h == 0 and hf == 0) else (),
                                  inc="pe" if (h == 23 and hf == 1) else None)
                if j == 7:
                    dn_last[g] = mm
                esl = evac[:, (j % 2) * 1024:((j % 2) + 1) * 1024]
                et = P.em(act, lambda e, o=esl, i=pd[pr][:, :]:
                          e.activation(out=o, in_=i, func=AF.Copy),
                          waits=[("pe", mm),
                                 out_dma.get((g, j - 2),
                                             out_dma.get((g - 1, 6 + (j % 2)))),
                                 xn_dma[(NB - 1, 2 + (j % 2))] if (g == 0 and j < 2)
                                 else None],
                          inc="act")
                dn_ev_tok[(g, j)] = et
                rl = ring_last.get(r)
                extra = [("act", et)] + ([("act", rl[0]), ("dve", rl[1])]
                                         if rl else [])
                ag, rs = ln_stats(r, esl, extra)
                ts = P.em(dve, lambda e, o=dn2[:], i=esl, m=st_mean(r),
                          rr=st_rstd(r):
                          e.tensor_scalar(out=o, in0=i, scalar1=m, scalar2=rr,
                                          op0=ALU.subtract, op1=ALU.mult),
                          waits=[("act", rs), ("dve", ag)], inc="dve")
                ring_last[r] = (rs, ts)
                at = P.em(dve, lambda e, o=esl, i=dn2[:], x=xsl:
                          e.tensor_add(out=o, in0=i, in1=x),
                          waits=[("dve", ts), xl], inc="dve")
                add3_tok[(g, j)] = at
                ssem = "sxa" if j % 2 == 0 else "sxb"
                out_dma[(g, j)] = (ssem, P.em(
                    pool, lambda e, o=out[s, tile * 128:(tile + 1) * 128, :],
                    i=esl: e.dma_start(out=o, in_=i),
                    waits=[("dve", at)], inc=ssem, amt=16))

        p3s.close()
        P.fence()

    store_sems = ("sxa", "sxb", "qt", "ks", "psa", "psb")
    fw = [(sn, P.cnt[sn]) for sn in store_sems]
    P.em(pool, None, waits=fw)
    P.em(syn, None, waits=fw)

    es.close()

    semstack = contextlib.ExitStack()
    SH = {sn: semstack.enter_context(nc.semaphore(sn)) for sn in SEMS}

    with nc.Block() as block:
        def runner(ename):
            def run(eng):
                for waits, fn, incs in P.ops[ename]:
                    for (sn, v) in waits:
                        eng.wait_ge(SH[sn], v)
                    if fn is None:
                        continue
                    ins = fn(eng)
                    for (sn, amt) in incs:
                        ins = ins.then_inc(SH[sn], amt)
            return run
        block.tensor(runner("tensor"))
        block.scalar(runner("scalar"))
        block.vector(runner("vector"))
        block.gpsimd(runner("gpsimd"))
        block.sync(runner("sync"))

    semstack.close()
    return nc


_NC_CACHE = None


def kernel(L0_attn_t, L0_attn_W_qkv, L0_attn_b_qkv,
           L0_ffn_w_gate, L0_ffn_w_up, L0_ffn_w_down):
    global _NC_CACHE
    if _NC_CACHE is None:
        _NC_CACHE = build_nc()
    nc = _NC_CACHE

    t = np.ascontiguousarray(L0_attn_t, dtype=np.float32)
    wq = np.ascontiguousarray(L0_attn_W_qkv, dtype=np.float32)
    bqv = np.ascontiguousarray(L0_attn_b_qkv, dtype=np.float32)
    wg = np.ascontiguousarray(L0_ffn_w_gate, dtype=np.int32)
    wu = np.ascontiguousarray(L0_ffn_w_up, dtype=np.int32)
    wd = np.ascontiguousarray(L0_ffn_w_down, dtype=np.int32)

    in_maps = [{"t": t[2 * c: 2 * c + 2], "wqkv": wq, "bqkv": bqv,
                "wg": wg, "wu": wu, "wd": wd} for c in range(8)]
    res = run_bass_kernel_spmd(nc, in_maps, core_ids=list(range(8)))
    return np.concatenate([r["out"] for r in res.results], axis=0)


# version 15 (final)
# speedup vs baseline: 1.0448x; 1.0448x over previous
import numpy as np
import contextlib
import concourse.bass as bass
import concourse.mybir as mybir
from concourse.bass_utils import run_bass_kernel_spmd

F32 = mybir.dt.float32
BF16 = mybir.dt.bfloat16
I32 = mybir.dt.int32
AF = mybir.ActivationFunctionType
ALU = mybir.AluOpType
AX = mybir.AxisListType

S, D, HID, E3 = 4096, 1024, 3072, 3072
NSEQ, NT, NG1, NB, NG3 = 2, 32, 8, 8, 4
EPS = 1e-5
ISQ = 1.0 / 32.0

ENGINES = ("tensor", "scalar", "vector", "gpsimd", "sync")
SEMS = ("pe", "act", "dve", "pool",
        "lda", "ldb", "ldc", "ldd",
        "wla", "wlb", "qa", "qb", "ka", "kb", "xba", "xbb",
        "sxa", "sxb", "qt", "ks", "misc", "psa", "psb", "qtr", "ksr")


class Prog:
    def __init__(self):
        self.ops = {e: [] for e in ENGINES}
        self.cnt = {s: 0 for s in SEMS}
        self.hw = {e: {s: 0 for s in SEMS} for e in ENGINES}

    def em(self, eng, fn, waits=(), inc=None, amt=1):
        w = []
        for item in waits:
            if item is None:
                continue
            sn, v = item
            if v is None or v <= 0 or self.hw[eng][sn] >= v:
                continue
            self.hw[eng][sn] = v
            w.append((sn, v))
        tok = None
        incs = []
        if inc is not None:
            self.cnt[inc] += amt
            tok = self.cnt[inc]
            incs.append((inc, amt))
        self.ops[eng].append((w, fn, incs))
        return tok

    FENCE_SEMS = ("pe", "act", "dve", "pool", "sxa", "sxb", "qt", "ks",
                  "psa", "psb")

    def fence(self):
        snap = [(sn, self.cnt[sn]) for sn in self.FENCE_SEMS]
        for e in ENGINES:
            self.em(e, None, waits=snap)


def build_nc(dbg=False):
    nc = bass.Bass("TRN2", target_bir_lowering=False)
    P = Prog()
    pe, act, dve, pool, syn = "tensor", "scalar", "vector", "gpsimd", "sync"

    t_in = nc.dram_tensor("t", [NSEQ, S, D], F32, kind="ExternalInput").ap()
    wqkv = nc.dram_tensor("wqkv", [E3, D], F32, kind="ExternalInput").ap()
    bqkv = nc.dram_tensor("bqkv", [E3], F32, kind="ExternalInput").ap()
    wg_i = nc.dram_tensor("wg", [HID, D], I32, kind="ExternalInput").ap()
    wu_i = nc.dram_tensor("wu", [HID, D], I32, kind="ExternalInput").ap()
    wd_i = nc.dram_tensor("wd", [D, HID], I32, kind="ExternalInput").ap()
    out = nc.dram_tensor("out", [NSEQ, S, D], F32, kind="ExternalOutput").ap()

    wqkvT = nc.dram_tensor("wqkvT", [24, 8, 128, 128], BF16, kind="Internal").ap()
    wgT = nc.dram_tensor("wgT", [24, 8, 128, 128], BF16, kind="Internal").ap()
    wuT = nc.dram_tensor("wuT", [24, 8, 128, 128], BF16, kind="Internal").ap()
    wdT = nc.dram_tensor("wdT", [24, 128, 1024], BF16, kind="Internal").ap()
    scratch_kind = "ExternalOutput" if dbg else "Internal"
    qscr = [nc.dram_tensor(f"qscr{i}", [NB, 8, 128, 512], BF16, kind=scratch_kind).ap()
            for i in range(NSEQ)]
    kscr = [nc.dram_tensor(f"kscr{i}", [NT, 8, 128, 128], BF16, kind=scratch_kind).ap()
            for i in range(NSEQ)]
    xnew = [nc.dram_tensor(f"xnew{i}", [NT, 128, 1024], F32, kind=scratch_kind).ap()
            for i in range(NSEQ)]
    if dbg:
        d_tiny = nc.dram_tensor("d_tiny", [1, 8], F32, kind="ExternalOutput").ap()
        d_lnxt = nc.dram_tensor("d_lnxt", [128, 4096], BF16, kind="ExternalOutput").ap()
        d_probs = nc.dram_tensor("d_probs", [128, 16384], BF16, kind="ExternalOutput").ap()
        d_recip = nc.dram_tensor("d_recip", [1, 512], F32, kind="ExternalOutput").ap()
        d_scale = nc.dram_tensor("d_scale", [128, 4], F32, kind="ExternalOutput").ap()
        d_x0 = nc.dram_tensor("d_x0", [128, 2048], F32, kind="ExternalOutput").ap()
        d_stats = nc.dram_tensor("d_stats", [128, 64], F32, kind="ExternalOutput").ap()
        d_ln0 = nc.dram_tensor("d_ln0", [128, 2048], BF16, kind="ExternalOutput").ap()

    es = contextlib.ExitStack()
    sb = lambda n, sh, dt: es.enter_context(nc.sbuf_tensor(n, sh, dt))

    varena = sb("varena", [128, 32 * 1024], BF16)
    parena = sb("parena", [128, 32 * 512], BF16)
    qarena = sb("qarena", [128, 2 * 4096], BF16)
    kbufs = sb("kbufs", [128, 8 * 1024], BF16)
    wbufs = sb("wbufs", [128, 2 * 4096], BF16)
    xa = sb("xa", [128, 4 * 1024], F32)
    xb = sb("xb", [128, 2 * 1024], F32)
    lnout = sb("lnout", [128, 4 * 1024], BF16)
    evac = sb("evac", [128, 2 * 1024], F32)
    dn2 = sb("dn2", [128, 1024], F32)
    silu_sb = sb("silu", [128, 1024], BF16)
    q2s = sb("q2s", [128, 8 * 512], BF16)
    stats = sb("stats", [128, 64], F32)
    ident = sb("ident", [128, 128], BF16)
    ident1 = sb("ident1", [1, 1], F32)
    ones_col = sb("ones_col", [128, 1], BF16)
    ones_row = sb("ones_row", [1, 512], BF16)
    ones_rf = sb("ones_rf", [1, 128], F32)
    epsv = sb("epsv", [128, 1], F32)
    bias_exp = sb("bias_exp", [128, 1], F32)
    tiny = sb("tiny", [1, 8], F32)
    recip_row = sb("recip_row", [1, 512], F32)

    bqT = sb("bqT", [128, 16], F32)
    bvrep = sb("bvrep", [128, 1024], F32)

    def st_sx(r, i): return stats[:, r * 24 + 6 * i: r * 24 + 6 * i + 6]
    def st_sext(r):  return stats[:, r * 24: r * 24 + 12]
    def st_mv(r):    return stats[:, r * 24 + 12: r * 24 + 14]
    def st_mean(r):  return stats[:, r * 24 + 12: r * 24 + 13]
    def st_var(r):   return stats[:, r * 24 + 13: r * 24 + 14]
    def st_lnv(r):   return stats[:, r * 24 + 14: r * 24 + 15]
    def st_rstd(r):  return stats[:, r * 24 + 15: r * 24 + 16]
    def st_negmr(r): return stats[:, r * 24 + 16: r * 24 + 17]
    def st_scale(j): return stats[:, 48 + j: 49 + j]

    def ln_stats(r, src_f32, extra_waits=()):
        b1 = P.em(dve, lambda e, o=st_sx(r, 0), i=src_f32[:, 0:512]:
                  e.bn_stats(out=o, in_=i), waits=list(extra_waits), inc="dve")
        b2 = P.em(dve, lambda e, o=st_sx(r, 1), i=src_f32[:, 512:1024]:
                  e.bn_stats(out=o, in_=i), inc="dve")
        ag = P.em(dve, lambda e, o=st_mv(r), i=st_sext(r):
                  e.bn_aggr(out=o, in_=i), waits=[("dve", b2)], inc="dve")
        lnt = P.em(act, lambda e, o=st_lnv(r), i=st_var(r):
                   e.activation(out=o, in_=i, func=AF.Ln, bias=epsv[:]),
                   waits=[("dve", ag)], inc="act")
        rs = P.em(act, lambda e, o=st_rstd(r), i=st_lnv(r):
                  e.activation(out=o, in_=i, func=AF.Exp, scale=-0.5),
                  waits=[("act", lnt)], inc="act")
        return ag, rs

    def _init0(g):
        return g.memset(ident[:], 0.0)
    init0_tok = P.em(pool, _init0, inc="pool")

    def _init(g):
        g.affine_select(out=ident[:], in_=ident[:], compare_op=ALU.not_equal,
                        fill=1.0, base=0, pattern=[[-1, 128]], channel_multiplier=1)
        g.memset(ident1[:], 1.0)
        g.memset(ones_col[:], 1.0)
        g.memset(ones_row[:], 1.0)
        g.memset(ones_rf[:], 1.0)
        g.memset(epsv[:], EPS)
        return g.memset(tiny[:], 0.0)
    init_tok = P.em(pool, _init, waits=[("pool", init0_tok)], inc="pool")

    pps = contextlib.ExitStack()
    ptr = [pps.enter_context(nc.psum_tensor(f"ptr{i}", [128, 1024], BF16))
           for i in range(2)]
    vf32 = varena.bitcast(F32)
    sstg = qarena
    cast_tok, tr_tok, cp_tok, stq_tok, ld_tok = {}, {}, {}, {}, {}
    sub_i = [0]

    def prep_chunk(ch, src_ap, nsub, dst_ap, stg_view, scale, in_i32):
        slot = ch % 4
        lsrc = vf32[:, slot * 4096: slot * 4096 + nsub * 1024]
        if in_i32:
            lsrc = lsrc.bitcast(I32)
        lsem = ("wla", "wlb", "qa", "qb")[slot]
        lt = (lsem, P.em(syn, lambda e, o=lsrc.rearrange(
                  "p (c d) -> p c d", c=nsub), i=src_ap:
                  e.dma_start(out=o, in_=i),
                  waits=[("act", cast_tok.get(ch - 4)), ("pool", init_tok)],
                  inc=lsem, amt=16))
        ld_tok[ch] = lt
        stg = sstg[:, (ch % 2) * 4096: (ch % 2) * 4096 + nsub * 1024]
        ct = pt = None
        for si in range(nsub):
            gsub = sub_i[0]
            sub_i[0] += 1
            lsl = gsub % 2
            csrc = lnout[:, lsl * 1024:(lsl + 1) * 1024]
            ct = P.em(act, lambda e, o=csrc,
                      i=lsrc[:, si * 1024:(si + 1) * 1024], sc=scale:
                      e.activation(out=o, in_=i, func=AF.Copy, scale=sc),
                      waits=[lt, ("pe", tr_tok.get(gsub - 2))], inc="act")
            tt = None
            for c in range(8):
                tt = P.em(pe, lambda e, o=ptr[lsl][:, c * 128:(c + 1) * 128],
                          i=csrc[:, c * 128:(c + 1) * 128]:
                          e.transpose(out=o, in_=i, identity=ident[:]),
                          waits=[("act", ct), ("dve", cp_tok.get(gsub - 2)),
                                 ("pool", init_tok)],
                          inc="pe" if c == 7 else None)
            tr_tok[gsub] = tt
            pt = P.em(dve, lambda e, o=stg[:, si * 1024:(si + 1) * 1024],
                      i=ptr[lsl][:, :]: e.tensor_copy(out=o, in_=i),
                      waits=[("pe", tt), stq_tok.get(ch - 2)], inc="dve")
            cp_tok[gsub] = pt
        cast_tok[ch] = ct
        psem = ("psa", "psb")[ch % 2]
        stq_tok[ch] = (psem, P.em(
            pool, lambda e, o=dst_ap, i=stg_view(stg):
            e.dma_start(out=o, in_=i),
            waits=[("dve", pt)], inc=psem, amt=16))

    ch = 0
    for wsrc, wdst, i32 in ((wqkv, wqkvT, False), (wg_i, wgT, True),
                            (wu_i, wuT, True)):
        for q in range(6):
            dst = wdst[4 * q: 4 * q + 4].rearrange("q c d k -> d q c k")
            prep_chunk(ch, wsrc[q * 512:(q + 1) * 512, :].rearrange(
                "(c p) d -> p c d", p=128), 4, dst,
                lambda st: st[:, 0:4096].rearrange("d (q2 c k) -> d q2 c k",
                                                   q2=4, c=8),
                1.0 if not i32 else 1 / 127.0, i32)
            ch += 1
    for dt_i in range(8):
        dst = wdT[0:24, :, dt_i * 128:(dt_i + 1) * 128].rearrange(
            "c d k -> d c k")
        prep_chunk(ch, wd_i[dt_i * 128:(dt_i + 1) * 128, :].rearrange(
            "p (c d) -> p c d", c=3), 3, dst,
            lambda st: st[:, 0:3072].rearrange("d (c k) -> d c k", c=24),
            1 / 127.0, True)
        ch += 1

    pbv = [pps.enter_context(nc.psum_tensor(f"pbv{i}", [128, 512], F32))
           for i in range(2)]
    bstg = parena.bitcast(F32)[0:16, 0:128]
    bstg2 = parena.bitcast(F32)[0:1, 1024:2048]
    b16 = lnout[0:16, 0:128]
    b16v = lnout[0:1, 1024:2048]
    bl = ("misc", P.em(syn, lambda e: e.dma_start(
        out=bstg, in_=bqkv[0:2048].rearrange("(c p) -> c p", p=128)),
        inc="misc", amt=16))
    blv = ("misc", P.em(syn, lambda e: e.dma_start(
        out=bstg2, in_=bqkv[2048:3072].rearrange("(a e) -> a e", a=1)),
        inc="misc", amt=16))
    c1 = P.em(act, lambda e: e.activation(out=b16, in_=bstg, func=AF.Copy),
              waits=[blv, ("pe", tr_tok[sub_i[0] - 2]), ("pool", init_tok)],
              inc="act")
    c2 = P.em(act, lambda e: e.activation(out=b16v, in_=bstg2, func=AF.Copy),
              waits=[blv, ("pe", tr_tok[sub_i[0] - 1])], inc="act")
    bt = P.em(pe, lambda e: e.transpose(out=ptr[0][:, 0:16], in_=b16,
                                        identity=ident[0:16, 0:16]),
              waits=[("act", c1), ("dve", cp_tok[sub_i[0] - 2])], inc="pe")
    bm0 = P.em(pe, lambda e: e.matmul(out=pbv[0][:, :], lhsT=ones_row[0:1, 0:128],
                                      rhs=b16v[:, 0:512], start=True, stop=True),
               waits=[("act", c2)], inc="pe")
    bm1 = P.em(pe, lambda e: e.matmul(out=pbv[1][:, :], lhsT=ones_row[0:1, 0:128],
                                      rhs=b16v[:, 512:1024], start=True, stop=True),
               inc="pe")
    bq_tok = P.em(act, lambda e: e.activation(out=bqT[:], in_=ptr[0][:, 0:16].bitcast(F32)[:, 0:16] if False else ptr[0][:, 0:16], func=AF.Copy),
                  waits=[("pe", bt)], inc="act")
    P.em(act, lambda e: e.activation(out=bvrep[:, 0:512], in_=pbv[0][:, :],
                                     func=AF.Copy), waits=[("pe", bm1)])
    bq_tok = P.em(act, lambda e: e.activation(out=bvrep[:, 512:1024],
                                              in_=pbv[1][:, :], func=AF.Copy),
                  inc="act")

    pps.close()
    P.fence()

    ring_last = {}

    for s in range(NSEQ):
        bq_gate = None
        p1s = contextlib.ExitStack()
        trP = [p1s.enter_context(nc.psum_tensor(f"s{s}trP{i}", [128, 1024], BF16))
               for i in range(2)]
        pqk = [p1s.enter_context(nc.psum_tensor(f"s{s}pqk{i}", [128, 512], F32))
               for i in range(2)]
        pv = [p1s.enter_context(nc.psum_tensor(f"s{s}pv{i}", [128, 512], F32))
              for i in range(2)]
        pqn = p1s.enter_context(nc.psum_tensor(f"s{s}pqn", [1, 512], F32))
        pkn = p1s.enter_context(nc.psum_tensor(f"s{s}pkn", [1, 512], F32))

        LNxT = parena[:, 0: 8 * 512]
        QTst = parena[:, 8 * 512: 16 * 512]
        Kst = parena[:, 16 * 512: 24 * 512]

        apply_tok, transp_tok, copy_tok = {}, {}, {}
        dbg_gate = {}
        ev_tok, vql_tok = {}, {}
        wset_tok = {}
        qn_tok, kn_tok, qmax_tok, kmax_tok = {}, {}, {}, {}
        vlast_tok = {}
        qt_dma, kscr_dma = {}, {}
        qt_relay, ks_relay = {}, {}
        sq2_tok = {}

        def wload(g, si):
            gs = g * 6 + si
            par = gs % 2
            dst = wbufs[:, par * 4096:(par + 1) * 4096].rearrange(
                "d (ee c k) -> d ee c k", ee=4, c=8)
            src = wqkvT[4 * si: 4 * si + 4].rearrange("ee c d k -> d ee c k")
            if si < 2:
                free = ("pe", vlast_tok.get(g - 1))
            else:
                free = ("pe", P.cnt["pe"])
            wsem = "wla" if par == 0 else "wlb"
            wset_tok[gs] = (wsem, P.em(syn, lambda e, o=dst, i=src:
                                e.dma_start(out=o, in_=i),
                                waits=[free], inc=wsem, amt=16))

        def chain_p1_load(tile):
            sl4 = tile % 4
            xsrc = xa[:, sl4 * 1024:(sl4 + 1) * 1024]
            lsem = ("lda", "ldb", "ldc", "ldd")[sl4]
            load_tok[tile] = (lsem, P.em(
                syn, lambda e, o=xsrc, i=t_in[s, tile * 128:(tile + 1) * 128, :]:
                e.dma_start(out=o, in_=i),
                waits=[apply_tok.get(tile - 4), dbg_gate.get(tile - 2)],
                inc=lsem, amt=16))

        def chain_p1_ln(tile):
            sl4 = tile % 4
            r = tile % 2
            xsrc = xa[:, sl4 * 1024:(sl4 + 1) * 1024]
            lo = lnout[:, sl4 * 1024:(sl4 + 1) * 1024]
            rl = ring_last.get(r)
            extra = [load_tok[tile]] + ([("act", rl[0]), ("dve", rl[1])]
                                        if rl else [])
            ag, rs = ln_stats(r, xsrc, extra)
            ap_ = P.em(dve, lambda e, o=lo, i=xsrc, m=st_mean(r), rr=st_rstd(r):
                       e.tensor_scalar(out=o, in0=i, scalar1=m, scalar2=rr,
                                       op0=ALU.subtract, op1=ALU.mult),
                       waits=[("act", rs), ("dve", ag),
                              ("pe", transp_tok.get(tile - 4))],
                       inc="dve")
            apply_tok[tile] = ("dve", ap_)
            ring_last[r] = (rs, ap_)

        load_tok = {}
        for tile in range(4):
            chain_p1_load(tile)
        for tile in range(4):
            chain_p1_ln(tile)

        for g in range(NG1):
            wload(g, 0)
            wload(g, 1)

            for j in range(4):
                tile = 4 * g + j
                sl = tile % 2
                sl4 = tile % 4
                lo = lnout[:, sl4 * 1024:(sl4 + 1) * 1024]
                ap_ = apply_tok[tile]
                tt = None
                for c in range(8):
                    tt = P.em(pe, lambda e, o=trP[sl][:, c * 128:(c + 1) * 128],
                              i=lo[:, c * 128:(c + 1) * 128]:
                              e.transpose(out=o, in_=i, identity=ident[:]),
                              waits=[ap_, ("dve", copy_tok.get(tile - 2))],
                              inc="pe" if c == 7 else None)
                transp_tok[tile] = tt
                dstv = LNxT.rearrange("d (c tk) -> d c tk", c=8)[
                    :, :, j * 128:(j + 1) * 128]
                srcv = trP[sl][:, :].rearrange("d (c tk) -> d c tk", c=8)
                copy_tok[tile] = P.em(
                    dve, lambda e, o=dstv, i=srcv: e.tensor_copy(out=o, in_=i),
                    waits=[("pe", tt), ("pe", vlast_tok.get(g - 1))], inc="dve")
            lnx_ready = copy_tok[4 * g + 3]

            for e_ in range(16):
                gs = g * 6 + e_ // 4
                par = gs % 2
                wv = wbufs[:, par * 4096:(par + 1) * 4096]
                bank = e_ % 2
                mwaits = [("dve", lnx_ready), wset_tok[gs],
                          ("act", ev_tok.get((g, e_ - 2),
                                             ev_tok.get((g - 1, e_ + 14)))),
                          ("act", bq_tok if g == 0 else None)]
                mmt = None
                for c in range(8):
                    lhs = wv[:, (e_ % 4) * 1024 + c * 128:
                             (e_ % 4) * 1024 + (c + 1) * 128]
                    rhs = LNxT[:, c * 512:(c + 1) * 512]
                    mmt = P.em(pe, lambda e, o=pqk[bank][:, :], l=lhs, rr=rhs,
                               st=(c == 0), sp=(c == 7):
                               e.matmul(out=o, lhsT=l, rhs=rr, start=st, stop=sp),
                               waits=mwaits if c == 0 else (),
                               inc="pe" if c == 7 else None)
                if e_ < 8:
                    dst = QTst[:, e_ * 512:(e_ + 1) * 512]
                    stage_free = qt_dma.get(g - 1)
                else:
                    dst = Kst[:, (e_ - 8) * 512:(e_ - 8 + 1) * 512]
                    stage_free = kscr_dma.get(g - 1)
                et = P.em(act, lambda e, o=dst, i=pqk[bank][:, :],
                          b=bqT[:, e_: e_ + 1]:
                          e.activation(out=o, in_=i, func=AF.Identity, bias=b),
                          waits=[("pe", mmt), stage_free], inc="act")
                ev_tok[(g, e_)] = et
                slq = e_ % 8
                sq_w = [("act", et)]
                if e_ < 8:
                    sq_w.append(("pe", kn_tok.get(g - 1)))
                else:
                    sq_w.append(("pe", qn_tok.get(g)))
                sq2_tok[(g, e_)] = P.em(
                    dve, lambda e, o=q2s[:, slq * 512:(slq + 1) * 512], i=dst:
                    e.tensor_mul(out=o, in0=i, in1=i), waits=sq_w, inc="dve")

                if e_ % 4 == 3:
                    wload(g, e_ // 4 + 2)
                if e_ == 3 and g + 1 < NG1:
                    for tl in range(4 * (g + 1), 4 * (g + 1) + 4):
                        chain_p1_load(tl)
                if e_ in (8, 10, 12, 14) and g + 1 < NG1:
                    chain_p1_ln(4 * (g + 1) + (e_ - 8) // 2)
                if e_ == 7:
                    tok = None
                    for c in range(8):
                        tok = P.em(pe, lambda e, o=pqn[:, :],
                                   rr=q2s[:, c * 512:(c + 1) * 512],
                                   st=(c == 0), sp=(c == 7):
                                   e.matmul(out=o, lhsT=ones_col[:], rhs=rr,
                                            start=st, stop=sp),
                                   waits=[("dve", sq2_tok[(g, 7)]),
                                          ("dve", qmax_tok.get(g - 1))],
                                   inc="pe" if c == 7 else None)
                    qn_tok[g] = tok
                    rm = P.em(dve, lambda e, i=pqn[:, :]: e.reduce_max(
                        out=tiny[:, 5:6], in_=i, axis=AX.X),
                        waits=[("pe", tok)], inc="dve")
                    qmax_tok[g] = P.em(dve, lambda e: e.tensor_max(
                        out=tiny[:, 0:1], in0=tiny[:, 0:1], in1=tiny[:, 5:6]),
                        waits=[("dve", rm)], inc="dve")

            for j in range(4):
                tile = 4 * g + j
                for hf in range(2):
                    gs = g * 6 + 4 + hf
                    par = gs % 2
                    wv4 = wbufs[:, par * 4096:(par + 1) * 4096].rearrange(
                        "d (ee c k) -> d ee c k", ee=4, c=8)
                    mwaits = [("dve", lnx_ready), wset_tok[gs],
                              ("dve", vql_tok.get((g, j - 1, hf),
                                                  vql_tok.get((g - 1, 3, hf))))]
                    mmt = None
                    for c in range(8):
                        lhs = LNxT[:, c * 512 + j * 128: c * 512 + (j + 1) * 128]
                        mmt = P.em(pe, lambda e, o=pv[hf][:, :], l=lhs,
                                   rr=wv4[:, :, c, :], st=(c == 0), sp=(c == 7):
                                   e.matmul(out=o, lhsT=l, rhs=rr,
                                            start=st, stop=sp),
                                   waits=mwaits if c == 0 else (),
                                   inc="pe" if c == 7 else None)
                    dst = varena[:, tile * 1024 + hf * 512:
                                 tile * 1024 + (hf + 1) * 512]
                    vql_tok[(g, j, hf)] = P.em(
                        dve, lambda e, o=dst, i=pv[hf][:, :],
                        b=bvrep[:, hf * 512:(hf + 1) * 512]:
                        e.tensor_add(out=o, in0=i, in1=b),
                        waits=[("pe", mmt)], inc="dve")
                if j == 0:
                        tok = None
                        for c in range(8):
                            tok = P.em(pe, lambda e, o=pkn[:, :],
                                       rr=q2s[:, c * 512:(c + 1) * 512],
                                       st=(c == 0), sp=(c == 7):
                                       e.matmul(out=o, lhsT=ones_col[:], rhs=rr,
                                                start=st, stop=sp),
                                       waits=[("dve", sq2_tok[(g, 15)]),
                                              ("dve", kmax_tok.get(g - 1))],
                                       inc="pe" if c == 7 else None)
                        kn_tok[g] = tok
                        rm = P.em(dve, lambda e, i=pkn[:, :]: e.reduce_max(
                            out=tiny[:, 6:7], in_=i, axis=AX.X),
                            waits=[("pe", tok)], inc="dve")
                        kmax_tok[g] = P.em(dve, lambda e: e.tensor_max(
                            out=tiny[:, 1:2], in0=tiny[:, 1:2], in1=tiny[:, 6:7]),
                            waits=[("dve", rm)], inc="dve")

            vlast_tok[g] = P.cnt["pe"]

            qt_dma[g] = ("qt", P.em(pool, lambda e,
                             o=qscr[s][g].rearrange("c d q -> d c q"),
                             i=QTst.rearrange("d (c q) -> d c q", c=8):
                             e.dma_start(out=o, in_=i),
                             waits=[("act", ev_tok[(g, 7)])], inc="qt", amt=16))
            qt_relay[g] = P.em(pool, lambda e: e.nop(),
                               waits=[("qt", P.cnt["qt"])], inc="qtr")
            tok = None
            for ec in range(8):
                ksrc = Kst[:, ec * 512:(ec + 1) * 512].rearrange(
                    "d (tt k) -> d tt k", tt=4)
                kdst = kscr[s][4 * g: 4 * g + 4, ec].rearrange("tt d k -> d tt k")
                tok = P.em(pool, lambda e, o=kdst, i=ksrc:
                           e.dma_start(out=o, in_=i),
                           waits=[("act", ev_tok[(g, 15)])], inc="ks", amt=16)
            kscr_dma[g] = ("ks", tok)
            ks_relay[g] = P.em(pool, lambda e: e.nop(),
                               waits=[("ks", P.cnt["ks"])], inc="ksr")

        P.em(act, lambda e: e.activation(out=tiny[:, 2:3], in_=tiny[:, 0:1],
                                         func=AF.Ln),
             waits=[("dve", qmax_tok[NG1 - 1]), ("dve", kmax_tok[NG1 - 1])])
        l2 = P.em(act, lambda e: e.activation(out=tiny[:, 3:4], in_=tiny[:, 1:2],
                                              func=AF.Ln), inc="act")
        ad = P.em(dve, lambda e: e.tensor_add(out=tiny[:, 2:3], in0=tiny[:, 2:3],
                                              in1=tiny[:, 3:4]),
                  waits=[("act", l2)], inc="dve")
        bb = P.em(act, lambda e: e.activation(out=tiny[:, 4:5], in_=tiny[:, 2:3],
                                              func=AF.Exp, scale=0.5),
                  waits=[("dve", ad)], inc="act")
        nb_ = P.em(dve, lambda e: e.tensor_scalar(
            out=tiny[:, 4:5], in0=tiny[:, 4:5], scalar1=-ISQ, scalar2=None,
            op0=ALU.mult), waits=[("act", bb)], inc="dve")
        bm = P.em(pe, lambda e, o=pv[0][:, 0:1]: e.matmul(
            out=o, lhsT=ones_rf[:], rhs=tiny[:, 4:5], start=True, stop=True),
                  waits=[("dve", nb_), ("dve", vql_tok[(NG1 - 1, 3, 0)])],
                  inc="pe")
        be = P.em(act, lambda e, i=pv[0][:, 0:1]: e.activation(
            out=bias_exp[:], in_=i, func=AF.Copy),
                  waits=[("pe", bm)], inc="act")

        if dbg and s == 0:
            P.em(pool, lambda e: e.dma_start(out=d_tiny, in_=tiny[:]),
                 waits=[("act", be)], inc="misc", amt=16)
            P.em(pool, lambda e: e.dma_start(out=d_lnxt, in_=parena[:, 0:4096]),
                 waits=[("pe", vlast_tok[NG1 - 1])], inc="misc", amt=16)

        p1s.close()

        p2s = contextlib.ExitStack()
        psc = [p2s.enter_context(nc.psum_tensor(f"s{s}psc{i}", [128, 512], F32))
               for i in range(3)]
        pav = [p2s.enter_context(nc.psum_tensor(f"s{s}pav{i}", [128, 1024], F32))
               for i in range(2)]
        psb7 = p2s.enter_context(nc.psum_tensor(f"s{s}psb7", [128, 512], F32))

        exp_tok, sc_tok = {}, {}
        rc_last = {}
        probs_free = {}
        q_tok, q_free = {}, {}
        kset_tok, kset_free = {}, {}
        evac_tok, add_tok, xn_dma = {}, {}, {}
        scale_cp_tok = {}

        def emit_av_phase(B):
            stok = None
            for ti in range(32):
                stok = P.em(pe, lambda e, o=psb7[0:1, :],
                            rr=parena[:, ti * 512:(ti + 1) * 512],
                            st=(ti == 0), sp=(ti == 31):
                            e.matmul(out=o, lhsT=ones_col[:], rhs=rr,
                                     start=st, stop=sp),
                            waits=([("act", exp_tok[(B, 31)]),
                                    ("act", scale_cp_tok.get(B - 1))]
                                   if ti == 0 else ()),
                            inc="pe" if ti == 31 else None)
            rc = P.em(dve, lambda e, i=psb7[0:1, :]: e.reciprocal(
                out=recip_row[:], in_=i),
                      waits=[("pe", stok)], inc="dve")
            rc_last["rc"] = rc
            ttok = None
            for j in range(4):
                ttok = P.em(pe, lambda e, o=psb7[:, 4 + j: 5 + j],
                            i=recip_row[0:1, j * 128:(j + 1) * 128]:
                            e.transpose(out=o, in_=i, identity=ident1[:]),
                            waits=[("dve", rc)], inc="pe" if j == 3 else None)
            sct = None
            for j in range(4):
                sct = P.em(act, lambda e, o=st_scale(j), i=psb7[:, 4 + j: 5 + j]:
                           e.activation(out=o, in_=i, func=AF.Copy),
                           waits=[("pe", ttok)], inc="act")
            scale_cp_tok[B] = sct
            for j in range(4):
                pr = j % 2
                tile = 4 * B + j
                r = j % 2
                w0 = [("act", exp_tok[(B, 31)]),
                      ("act", evac_tok.get((B, j - 2),
                                           evac_tok.get((B - 1, 2 + (j % 2)))))]
                av1 = None
                for hf in range(2):
                    for ti in range(32):
                        lhs = parena[:, ti * 512 + j * 128:
                                     ti * 512 + (j + 1) * 128]
                        rhs = varena[:, ti * 1024 + hf * 512:
                                     ti * 1024 + (hf + 1) * 512]
                        av1 = P.em(pe, lambda e,
                                   o=pav[pr][:, hf * 512:(hf + 1) * 512],
                                   l=lhs, rr=rhs, st=(ti == 0), sp=(ti == 31):
                                   e.matmul(out=o, lhsT=l, rhs=rr,
                                            start=st, stop=sp),
                                   waits=w0 if (ti == 0 and hf == 0) else (),
                                   inc="pe" if ti == 31 else None)
                xsl = xb[:, (j % 2) * 1024:((j % 2) + 1) * 1024]
                xsem = "xba" if j % 2 == 0 else "xbb"
                xl = (xsem, P.em(syn, lambda e, o=xsl,
                          i=t_in[s, tile * 128:(tile + 1) * 128, :]:
                          e.dma_start(out=o, in_=i),
                          waits=[("dve", add_tok.get((B, j - 2),
                                                     add_tok.get((B - 1, 2 + (j % 2)))))],
                          inc=xsem, amt=16))
                esl = evac[:, (j % 2) * 1024:((j % 2) + 1) * 1024]
                et = P.em(act, lambda e, o=esl, i=pav[pr][:, :], sc=st_scale(j):
                          e.activation(out=o, in_=i, func=AF.Copy, scale=sc),
                          waits=[("pe", av1), ("act", sct),
                                 xn_dma.get((B, j - 2),
                                            xn_dma.get((B - 1, 2 + (j % 2))))],
                          inc="act")
                evac_tok[(B, j)] = et
                rl = ring_last.get(r)
                extra = [("act", et)] + ([("act", rl[0]), ("dve", rl[1])]
                                         if rl else [])
                ag, rs = ln_stats(r, esl, extra)
                ts = P.em(dve, lambda e, o=dn2[:], i=esl, m=st_mean(r),
                          rr=st_rstd(r):
                          e.tensor_scalar(out=o, in0=i, scalar1=m, scalar2=rr,
                                          op0=ALU.subtract, op1=ALU.mult),
                          waits=[("act", rs), ("dve", ag)], inc="dve")
                ring_last[r] = (rs, ts)
                at = P.em(dve, lambda e, o=esl, i=dn2[:], x=xsl:
                          e.tensor_add(out=o, in0=i, in1=x),
                          waits=[("dve", ts), xl], inc="dve")
                add_tok[(B, j)] = at
                ssem = "sxa" if j % 2 == 0 else "sxb"
                xn_dma[(B, j)] = (ssem, P.em(pool, lambda e, o=xnew[s][tile], i=esl:
                                      e.dma_start(out=o, in_=i),
                                      waits=[("dve", at)], inc=ssem, amt=16))
                if j == 3:
                    probs_free[B] = av1

        for b in range(NB):
            qdst = qarena[:, (b % 2) * 4096:((b % 2) + 1) * 4096].rearrange(
                "d (c q) -> d c q", c=8)
            qsem = "qa" if b % 2 == 0 else "qb"
            q_tok[b] = (qsem, P.em(syn, lambda e, o=qdst,
                            i=qscr[s][b].rearrange("c d q -> d c q"):
                            e.dma_start(out=o, in_=i),
                            waits=[("pe", q_free.get(b - 2)),
                                   ("qtr", qt_relay[b])],
                            inc=qsem, amt=16))
            if b > 0:
                emit_av_phase(b - 1)
            for ti in range(32):
                if ti % 4 == 0:
                    si = ti // 4
                    kdst = kbufs[:, (si % 2) * 4096:((si % 2) + 1) * 4096].rearrange(
                        "d (tt c k) -> d tt c k", tt=4, c=8)
                    ksrc = kscr[s][4 * si: 4 * si + 4].rearrange(
                        "tt c d k -> d tt c k")
                    ksem = "ka" if si % 2 == 0 else "kb"
                    kset_tok[(b, si)] = (ksem, P.em(
                        syn, lambda e, o=kdst, i=ksrc: e.dma_start(out=o, in_=i),
                        waits=[("pe", kset_free.get((b, si - 2),
                                                    kset_free.get((b - 1, 6 + si)))),
                               ("ksr", ks_relay[si])],
                        inc=ksem, amt=16))
                bank = ti % 3
                if ti >= 3:
                    prev_exp = exp_tok[(b, ti - 3)]
                else:
                    prev_exp = exp_tok.get((b - 1, {0: 30, 1: 31, 2: 29}[ti]))
                w0 = [kset_tok[(b, ti // 4)], q_tok[b],
                      ("act", prev_exp), ("act", be),
                      ("qtr", qt_relay[NG1 - 1]), ("ksr", ks_relay[NG1 - 1])]
                mm = None
                for c in range(8):
                    lhs = kbufs[:, (((ti // 4) % 2) * 4 + ti % 4) * 1024 + c * 128:
                                (((ti // 4) % 2) * 4 + ti % 4) * 1024 + (c + 1) * 128]
                    rhs = qarena[:, (b % 2) * 4096 + c * 512:
                                 (b % 2) * 4096 + (c + 1) * 512]
                    mm = P.em(pe, lambda e, o=psc[bank][:, :], l=lhs, rr=rhs,
                              st=(c == 0), sp=(c == 7):
                              e.matmul(out=o, lhsT=l, rhs=rr, start=st, stop=sp),
                              waits=w0 if c == 0 else (),
                              inc="pe" if c == 7 else None)
                sc_tok[(b, ti)] = mm
                if ti % 4 == 3:
                    kset_free[(b, ti // 4)] = mm
                exp_tok[(b, ti)] = P.em(
                    act, lambda e, o=parena[:, ti * 512:(ti + 1) * 512],
                    i=psc[bank][:, :]:
                    e.activation(out=o, in_=i, func=AF.Exp, scale=ISQ,
                                 bias=bias_exp[:]),
                    waits=[("pe", mm)], inc="act")
            q_free[b] = sc_tok[(b, 31)]
        emit_av_phase(NB - 1)

        if dbg and s == 0:
            P.em(pool, lambda e: e.dma_start(out=d_probs, in_=parena[:, :]),
                 waits=[("act", exp_tok[(NB - 1, 31)])], inc="misc", amt=16)
            P.em(pool, lambda e: e.dma_start(out=d_recip, in_=recip_row[:]),
                 waits=[("dve", rc_last["rc"])], inc="misc", amt=16)
            P.em(pool, lambda e: e.dma_start(out=d_scale, in_=stats[:, 48:52]),
                 waits=[("act", scale_cp_tok[NB - 1])], inc="misc", amt=16)

        p2s.close()
        p2_act_done = evac_tok[(NB - 1, 3)]
        p2_av_done = probs_free[NB - 1]

        p3s = contextlib.ExitStack()
        pg = p3s.enter_context(nc.psum_tensor(f"s{s}pg", [128, 1024], F32))
        pu = p3s.enter_context(nc.psum_tensor(f"s{s}pu", [128, 1024], F32))
        pd = [p3s.enter_context(nc.psum_tensor(f"s{s}pd{i}", [128, 1024], F32))
              for i in range(2)]
        trv = pg[:, :].bitcast(BF16)
        LNx2T = varena[:, 0: 8 * 1024]

        def hact_ap(h):
            if h < 16:
                return parena[:, h * 1024:(h + 1) * 1024]
            return qarena[:, (h - 16) * 1024:(h - 16 + 1) * 1024]

        apply3_tok, transp3_tok, copy3_tok = {}, {}, {}
        silu_tok, hact_tok, gate_tok, up_tok = {}, {}, {}, {}
        wpair_tok, wpair_free = {}, {}
        dn_ev_tok, add3_tok, out_dma = {}, {}, {}
        gu_last, dn_last = {}, {}

        load3_tok = {}

        def chain_p3_load(g, j):
            tile = 8 * g + j
            sl4 = j % 4
            xsrc = xa[:, sl4 * 1024:(sl4 + 1) * 1024]
            lsem = ("lda", "ldb", "ldc", "ldd")[sl4]
            xw = [apply3_tok.get((g, j - 4), apply3_tok.get((g - 1, j + 4))),
                  xn_dma[(tile // 4, tile % 4)]]
            if g == 0 and j < 4:
                xw.append(apply_tok.get(28 + j))
            load3_tok[(g, j)] = (lsem, P.em(
                syn, lambda e, o=xsrc, i=xnew[s][tile]:
                e.dma_start(out=o, in_=i), waits=xw, inc=lsem, amt=16))

        def chain_p3_ln(g, j):
            sl4 = j % 4
            r = j % 2
            xsrc = xa[:, sl4 * 1024:(sl4 + 1) * 1024]
            lo = lnout[:, sl4 * 1024:(sl4 + 1) * 1024]
            rl = ring_last.get(r)
            extra = [load3_tok[(g, j)]] + ([("act", rl[0]), ("dve", rl[1])]
                                           if rl else [])
            ag, rs = ln_stats(r, xsrc, extra)
            aw = [("act", rs), ("dve", ag),
                  ("pe", transp3_tok.get((g, j - 4),
                                         transp3_tok.get((g - 1, j + 4))))]
            if g == 0 and j < 4:
                aw.append(("pe", transp_tok[28 + j]))
            ap_ = P.em(dve, lambda e, o=lo, i=xsrc, m=st_mean(r), rr=st_rstd(r):
                       e.tensor_scalar(out=o, in0=i, scalar1=m, scalar2=rr,
                                       op0=ALU.subtract, op1=ALU.mult),
                       waits=aw, inc="dve")
            apply3_tok[(g, j)] = ("dve", ap_)
            ring_last[r] = (rs, ap_)

        for jj in range(4):
            chain_p3_load(0, jj)
        wd_tok = None
        for q4 in range(6):
            ddst = varena[:, (8 + 4 * q4) * 1024:(8 + 4 * q4 + 4) * 1024].rearrange(
                "d (c k) -> d c k", c=4)
            wd_tok = ("misc", P.em(syn, lambda e, o=ddst,
                          i=wdT[4 * q4: 4 * q4 + 4].rearrange("c d k -> d c k"):
                          e.dma_start(out=o, in_=i),
                          waits=[("pe", p2_av_done)], inc="misc", amt=16))


        def p3_transp(g, j):
                sl4 = j % 4
                lo = lnout[:, sl4 * 1024:(sl4 + 1) * 1024]
                ap_ = apply3_tok[(g, j)]
                sl = j % 2
                tt = None
                for c in range(8):
                    tt = P.em(pe, lambda e,
                              o=trv[:, sl * 1024 + c * 128: sl * 1024 + (c + 1) * 128],
                              i=lo[:, c * 128:(c + 1) * 128]:
                              e.transpose(out=o, in_=i, identity=ident[:]),
                              waits=[ap_, ("dve", copy3_tok.get((g, j - 2))),
                                     ("pe", gu_last.get(g - 1)),
                                     ("act", silu_tok.get((g - 1, 23))),
                                     ("act", exp_tok[(NB - 1, 31)]
                                      if g == 0 else None)],
                              inc="pe" if c == 7 else None)
                transp3_tok[(g, j)] = tt
                dstv = LNx2T.rearrange("d (c tk) -> d c tk", c=8)[
                    :, :, j * 128:(j + 1) * 128]
                srcv = trv[:, sl * 1024:(sl + 1) * 1024].rearrange(
                    "d (c tk) -> d c tk", c=8)
                copy3_tok[(g, j)] = P.em(
                    dve, lambda e, o=dstv, i=srcv: e.tensor_copy(out=o, in_=i),
                    waits=[("pe", tt), ("pe", gu_last.get(g - 1))], inc="dve")

        for g in range(NG3):
            for j in range(4):
                if (g, j) not in load3_tok:
                    chain_p3_load(g, j)
            for j in range(4):
                if (g, j) not in apply3_tok:
                    chain_p3_ln(g, j)
            for j in range(4, 8):
                if (g, j) not in load3_tok:
                    chain_p3_load(g, j)
            for j in range(4):
                p3_transp(g, j)
            for j in range(4, 8):
                chain_p3_ln(g, j)
            for j in range(4, 8):
                p3_transp(g, j)
            lnx2_ready = copy3_tok[(g, 7)]

            for h in range(24):
                pi = h // 2
                par = pi % 2
                hh = h % 2
                if hh == 0:
                    wsem = "wla" if par == 0 else "wlb"
                    for wi, (wt, off) in enumerate([(wgT, 0), (wuT, 2048)]):
                        ddst = wbufs[:, par * 4096 + off: par * 4096 + off + 2048]
                        ddst = ddst.rearrange("d (hc c k) -> d hc c k", hc=2, c=8)
                        tok = P.em(
                            syn, lambda e, o=ddst,
                            i=wt[h:h + 2].rearrange("hc c d k -> d hc c k"):
                            e.dma_start(out=o, in_=i),
                            waits=[("pe", wpair_free.get((g, pi - 2),
                                                         wpair_free.get((g - 1, 10 + pi)))),
                                   ("pe", vlast_tok[NG1 - 1] if g == 0 else None)],
                            inc=wsem, amt=16)
                    wpair_tok[(g, pi)] = (wsem, tok)
                gw = [("dve", lnx2_ready), wpair_tok[(g, pi)],
                      ("act", silu_tok.get((g, h - 1), silu_tok.get((g - 1, 23))))]
                mm = None
                for hf in range(2):
                    for c in range(8):
                        lhs = wbufs[:, par * 4096 + hh * 1024 + c * 128:
                                    par * 4096 + hh * 1024 + (c + 1) * 128]
                        rhs = LNx2T[:, c * 1024 + hf * 512: c * 1024 + (hf + 1) * 512]
                        mm = P.em(pe, lambda e, o=pg[:, hf * 512:(hf + 1) * 512],
                                  l=lhs, rr=rhs, st=(c == 0), sp=(c == 7):
                                  e.matmul(out=o, lhsT=l, rhs=rr, start=st, stop=sp),
                                  waits=gw if (c == 0 and hf == 0) else (),
                                  inc="pe" if (c == 7 and hf == 1) else None)
                gate_tok[(g, h)] = mm
                uw = [wpair_tok[(g, pi)],
                      ("dve", hact_tok.get((g, h - 1), hact_tok.get((g - 1, 23)))),
                      ("act", p2_act_done if (g == 0 and h == 0) else None)]
                for hf in range(2):
                    for c in range(8):
                        lhs = wbufs[:, par * 4096 + 2048 + hh * 1024 + c * 128:
                                    par * 4096 + 2048 + hh * 1024 + (c + 1) * 128]
                        rhs = LNx2T[:, c * 1024 + hf * 512: c * 1024 + (hf + 1) * 512]
                        mm = P.em(pe, lambda e, o=pu[:, hf * 512:(hf + 1) * 512],
                                  l=lhs, rr=rhs, st=(c == 0), sp=(c == 7):
                                  e.matmul(out=o, lhsT=l, rhs=rr, start=st, stop=sp),
                                  waits=uw if (c == 0 and hf == 0) else (),
                                  inc="pe" if (c == 7 and hf == 1) else None)
                up_tok[(g, h)] = mm
                if hh == 1:
                    wpair_free[(g, pi)] = mm
                if h == 16 and g + 1 < NG3:
                    for jn in range(4):
                        chain_p3_load(g + 1, jn)
                if h == 23:
                    gu_last[g] = mm
                st_ = P.em(act, lambda e, o=silu_sb[:], i=pg[:, :]:
                           e.activation(out=o, in_=i, func=AF.Silu),
                           waits=[("pe", gate_tok[(g, h)]),
                                  ("dve", hact_tok.get((g, h - 1),
                                                       hact_tok.get((g - 1, 23))))],
                           inc="act")
                silu_tok[(g, h)] = st_
                hact_tok[(g, h)] = P.em(
                    dve, lambda e, o=hact_ap(h), i=silu_sb[:], u=pu[:, :]:
                    e.tensor_mul(out=o, in0=i, in1=u),
                    waits=[("act", st_), ("pe", up_tok[(g, h)]),
                           ("pe", dn_last.get(g - 1))],
                    inc="dve")
                if h >= 20 and g + 1 < NG3:
                    chain_p3_ln(g + 1, h - 20)
            hact_ready = hact_tok[(g, 23)]

            for j in range(8):
                tile = 8 * g + j
                pr = j % 2
                r = j % 2
                xsl = xb[:, (j % 2) * 1024:((j % 2) + 1) * 1024]
                xsem = "xba" if j % 2 == 0 else "xbb"
                xl = (xsem, P.em(syn, lambda e, o=xsl, i=xnew[s][tile]:
                          e.dma_start(out=o, in_=i),
                          waits=[("dve", add3_tok.get((g, j - 2),
                                                      add3_tok.get((g - 1, 6 + (j % 2))))),
                                 xn_dma[(tile // 4, tile % 4)]],
                          inc=xsem, amt=16))
                w0 = [("dve", hact_ready), wd_tok,
                      ("act", dn_ev_tok.get((g, j - 2),
                                            dn_ev_tok.get((g - 1, 6 + (j % 2))))),
                      ("act", p2_act_done if (g == 0 and j < 2) else None),
                      ("act", scale_cp_tok[NB - 1] if (g == 0 and j < 2) else None)]
                mm = None
                for hf in range(2):
                    for h in range(24):
                        lhs = hact_ap(h)[:, j * 128:(j + 1) * 128]
                        rhs = varena[:, (8 + h) * 1024 + hf * 512:
                                     (8 + h) * 1024 + (hf + 1) * 512]
                        mm = P.em(pe, lambda e,
                                  o=pd[pr][:, hf * 512:(hf + 1) * 512],
                                  l=lhs, rr=rhs, st=(h == 0), sp=(h == 23):
                                  e.matmul(out=o, lhsT=l, rhs=rr,
                                           start=st, stop=sp),
                                  waits=w0 if (h == 0 and hf == 0) else (),
                                  inc="pe" if (h == 23 and hf == 1) else None)
                if j == 7:
                    dn_last[g] = mm
                esl = evac[:, (j % 2) * 1024:((j % 2) + 1) * 1024]
                et = P.em(act, lambda e, o=esl, i=pd[pr][:, :]:
                          e.activation(out=o, in_=i, func=AF.Copy),
                          waits=[("pe", mm),
                                 out_dma.get((g, j - 2),
                                             out_dma.get((g - 1, 6 + (j % 2)))),
                                 xn_dma[(NB - 1, 2 + (j % 2))] if (g == 0 and j < 2)
                                 else None],
                          inc="act")
                dn_ev_tok[(g, j)] = et
                rl = ring_last.get(r)
                extra = [("act", et)] + ([("act", rl[0]), ("dve", rl[1])]
                                         if rl else [])
                ag, rs = ln_stats(r, esl, extra)
                ts = P.em(dve, lambda e, o=dn2[:], i=esl, m=st_mean(r),
                          rr=st_rstd(r):
                          e.tensor_scalar(out=o, in0=i, scalar1=m, scalar2=rr,
                                          op0=ALU.subtract, op1=ALU.mult),
                          waits=[("act", rs), ("dve", ag)], inc="dve")
                ring_last[r] = (rs, ts)
                at = P.em(dve, lambda e, o=esl, i=dn2[:], x=xsl:
                          e.tensor_add(out=o, in0=i, in1=x),
                          waits=[("dve", ts), xl], inc="dve")
                add3_tok[(g, j)] = at
                ssem = "sxa" if j % 2 == 0 else "sxb"
                out_dma[(g, j)] = (ssem, P.em(
                    pool, lambda e, o=out[s, tile * 128:(tile + 1) * 128, :],
                    i=esl: e.dma_start(out=o, in_=i),
                    waits=[("dve", at)], inc=ssem, amt=16))

        p3s.close()
        P.fence()

    store_sems = ("sxa", "sxb", "qt", "ks", "psa", "psb")
    fw = [(sn, P.cnt[sn]) for sn in store_sems]
    P.em(pool, None, waits=fw)
    P.em(syn, None, waits=fw)

    es.close()

    semstack = contextlib.ExitStack()
    SH = {sn: semstack.enter_context(nc.semaphore(sn)) for sn in SEMS}

    with nc.Block() as block:
        def runner(ename):
            def run(eng):
                for waits, fn, incs in P.ops[ename]:
                    for (sn, v) in waits:
                        eng.wait_ge(SH[sn], v)
                    if fn is None:
                        continue
                    ins = fn(eng)
                    for (sn, amt) in incs:
                        ins = ins.then_inc(SH[sn], amt)
            return run
        block.tensor(runner("tensor"))
        block.scalar(runner("scalar"))
        block.vector(runner("vector"))
        block.gpsimd(runner("gpsimd"))
        block.sync(runner("sync"))

    semstack.close()
    return nc


_NC_CACHE = None


def kernel(L0_attn_t, L0_attn_W_qkv, L0_attn_b_qkv,
           L0_ffn_w_gate, L0_ffn_w_up, L0_ffn_w_down):
    global _NC_CACHE
    if _NC_CACHE is None:
        _NC_CACHE = build_nc()
    nc = _NC_CACHE

    t = np.ascontiguousarray(L0_attn_t, dtype=np.float32)
    wq = np.ascontiguousarray(L0_attn_W_qkv, dtype=np.float32)
    bqv = np.ascontiguousarray(L0_attn_b_qkv, dtype=np.float32)
    wg = np.ascontiguousarray(L0_ffn_w_gate, dtype=np.int32)
    wu = np.ascontiguousarray(L0_ffn_w_up, dtype=np.int32)
    wd = np.ascontiguousarray(L0_ffn_w_down, dtype=np.int32)

    in_maps = [{"t": t[2 * c: 2 * c + 2], "wqkv": wq, "bqkv": bqv,
                "wg": wg, "wu": wu, "wd": wd} for c in range(8)]
    res = run_bass_kernel_spmd(nc, in_maps, core_ids=list(range(8)))
    return np.concatenate([r["out"] for r in res.results], axis=0)


# version 16
# speedup vs baseline: 1.1244x; 1.0762x over previous
import numpy as np
import contextlib
import concourse.bass as bass
import concourse.mybir as mybir
from concourse.bass_utils import run_bass_kernel_spmd

F32 = mybir.dt.float32
BF16 = mybir.dt.bfloat16
I32 = mybir.dt.int32
AF = mybir.ActivationFunctionType
ALU = mybir.AluOpType
AX = mybir.AxisListType

S, D, HID, E3 = 4096, 1024, 3072, 3072
NSEQ, NT, NG1, NB, NG3 = 2, 32, 8, 8, 4
EPS = 1e-5
ISQ = 1.0 / 32.0

ENGINES = ("tensor", "scalar", "vector", "gpsimd", "sync")
SEMS = ("pe", "act", "dve", "pool",
        "lda", "ldb", "ldc", "ldd",
        "wla", "wlb", "qa", "qb", "ka", "kb", "xba", "xbb",
        "sxa", "sxb", "qt", "ks", "misc", "psa", "psb", "qtr", "ksr")


class Prog:
    def __init__(self):
        self.ops = {e: [] for e in ENGINES}
        self.cnt = {s: 0 for s in SEMS}
        self.hw = {e: {s: 0 for s in SEMS} for e in ENGINES}

    def em(self, eng, fn, waits=(), inc=None, amt=1):
        w = []
        for item in waits:
            if item is None:
                continue
            sn, v = item
            if v is None or v <= 0 or self.hw[eng][sn] >= v:
                continue
            self.hw[eng][sn] = v
            w.append((sn, v))
        tok = None
        incs = []
        if inc is not None:
            self.cnt[inc] += amt
            tok = self.cnt[inc]
            incs.append((inc, amt))
        self.ops[eng].append((w, fn, incs))
        return tok

    FENCE_SEMS = ("pe", "act", "dve", "pool", "sxa", "sxb", "qt", "ks",
                  "psa", "psb")

    def fence(self):
        snap = [(sn, self.cnt[sn]) for sn in self.FENCE_SEMS]
        for e in ENGINES:
            self.em(e, None, waits=snap)


def build_nc(dbg=False):
    nc = bass.Bass("TRN2", target_bir_lowering=False)
    P = Prog()
    pe, act, dve, pool, syn = "tensor", "scalar", "vector", "gpsimd", "sync"

    t_in = nc.dram_tensor("t", [NSEQ, S, D], F32, kind="ExternalInput").ap()
    wqkv = nc.dram_tensor("wqkv", [E3, D], F32, kind="ExternalInput").ap()
    bqkv = nc.dram_tensor("bqkv", [E3], F32, kind="ExternalInput").ap()
    wg_i = nc.dram_tensor("wg", [HID, D], I32, kind="ExternalInput").ap()
    wu_i = nc.dram_tensor("wu", [HID, D], I32, kind="ExternalInput").ap()
    wd_i = nc.dram_tensor("wd", [D, HID], I32, kind="ExternalInput").ap()
    out = nc.dram_tensor("out", [NSEQ, S, D], F32, kind="ExternalOutput").ap()

    wqkvT = nc.dram_tensor("wqkvT", [24, 8, 128, 128], BF16, kind="Internal").ap()
    wgT = nc.dram_tensor("wgT", [24, 8, 128, 128], BF16, kind="Internal").ap()
    wuT = nc.dram_tensor("wuT", [24, 8, 128, 128], BF16, kind="Internal").ap()
    wdT = nc.dram_tensor("wdT", [24, 128, 1024], BF16, kind="Internal").ap()
    scratch_kind = "ExternalOutput" if dbg else "Internal"
    qscr = [nc.dram_tensor(f"qscr{i}", [NB, 8, 128, 512], BF16, kind=scratch_kind).ap()
            for i in range(NSEQ)]
    kscr = [nc.dram_tensor(f"kscr{i}", [NT, 8, 128, 128], BF16, kind=scratch_kind).ap()
            for i in range(NSEQ)]
    xnew = [nc.dram_tensor(f"xnew{i}", [NT, 128, 1024], F32, kind=scratch_kind).ap()
            for i in range(NSEQ)]
    if dbg:
        d_tiny = nc.dram_tensor("d_tiny", [1, 8], F32, kind="ExternalOutput").ap()
        d_lnxt = nc.dram_tensor("d_lnxt", [128, 4096], BF16, kind="ExternalOutput").ap()
        d_probs = nc.dram_tensor("d_probs", [128, 16384], BF16, kind="ExternalOutput").ap()
        d_recip = nc.dram_tensor("d_recip", [1, 512], F32, kind="ExternalOutput").ap()
        d_scale = nc.dram_tensor("d_scale", [128, 4], F32, kind="ExternalOutput").ap()
        d_x0 = nc.dram_tensor("d_x0", [128, 2048], F32, kind="ExternalOutput").ap()
        d_stats = nc.dram_tensor("d_stats", [128, 64], F32, kind="ExternalOutput").ap()
        d_ln0 = nc.dram_tensor("d_ln0", [128, 2048], BF16, kind="ExternalOutput").ap()

    es = contextlib.ExitStack()
    sb = lambda n, sh, dt: es.enter_context(nc.sbuf_tensor(n, sh, dt))

    varena = sb("varena", [128, 32 * 1024], BF16)
    parena = sb("parena", [128, 32 * 512], BF16)
    qarena = sb("qarena", [128, 2 * 4096], BF16)
    kbufs = sb("kbufs", [128, 8 * 1024], BF16)
    wbufs = sb("wbufs", [128, 2 * 4096], BF16)
    xa = sb("xa", [128, 4 * 1024], F32)
    xb = sb("xb", [128, 2 * 1024], F32)
    lnout = sb("lnout", [128, 4 * 1024], BF16)
    evac = sb("evac", [128, 2 * 1024], F32)
    dn2 = sb("dn2", [128, 1024], F32)
    silu_sb = sb("silu", [128, 1024], BF16)
    q2s = sb("q2s", [128, 8 * 512], BF16)
    stats = sb("stats", [128, 64], F32)
    ident = sb("ident", [128, 128], BF16)
    ident1 = sb("ident1", [1, 1], F32)
    ones_col = sb("ones_col", [128, 1], BF16)
    ones_row = sb("ones_row", [1, 512], BF16)
    ones_rf = sb("ones_rf", [1, 128], F32)
    epsv = sb("epsv", [128, 1], F32)
    bias_exp = sb("bias_exp", [128, 1], F32)
    tiny = sb("tiny", [1, 8], F32)
    recip_row = sb("recip_row", [1, 512], F32)

    bqT = sb("bqT", [128, 16], F32)
    bvrep = sb("bvrep", [128, 1024], F32)

    def st_sx(r, i): return stats[:, r * 24 + 6 * i: r * 24 + 6 * i + 6]
    def st_sext(r):  return stats[:, r * 24: r * 24 + 12]
    def st_mv(r):    return stats[:, r * 24 + 12: r * 24 + 14]
    def st_mean(r):  return stats[:, r * 24 + 12: r * 24 + 13]
    def st_var(r):   return stats[:, r * 24 + 13: r * 24 + 14]
    def st_lnv(r):   return stats[:, r * 24 + 14: r * 24 + 15]
    def st_rstd(r):  return stats[:, r * 24 + 15: r * 24 + 16]
    def st_negmr(r): return stats[:, r * 24 + 16: r * 24 + 17]
    def st_scale(j): return stats[:, 48 + j: 49 + j]

    def ln_stats(r, src_f32, extra_waits=()):
        b1 = P.em(dve, lambda e, o=st_sx(r, 0), i=src_f32[:, 0:512]:
                  e.bn_stats(out=o, in_=i), waits=list(extra_waits), inc="dve")
        b2 = P.em(dve, lambda e, o=st_sx(r, 1), i=src_f32[:, 512:1024]:
                  e.bn_stats(out=o, in_=i), inc="dve")
        ag = P.em(dve, lambda e, o=st_mv(r), i=st_sext(r):
                  e.bn_aggr(out=o, in_=i), waits=[("dve", b2)], inc="dve")
        lnt = P.em(act, lambda e, o=st_lnv(r), i=st_var(r):
                   e.activation(out=o, in_=i, func=AF.Ln, bias=epsv[:]),
                   waits=[("dve", ag)], inc="act")
        rs = P.em(act, lambda e, o=st_rstd(r), i=st_lnv(r):
                  e.activation(out=o, in_=i, func=AF.Exp, scale=-0.5),
                  waits=[("act", lnt)], inc="act")
        return ag, rs

    def _init0(g):
        return g.memset(ident[:], 0.0)
    init0_tok = P.em(pool, _init0, inc="pool")

    def _init(g):
        g.affine_select(out=ident[:], in_=ident[:], compare_op=ALU.not_equal,
                        fill=1.0, base=0, pattern=[[-1, 128]], channel_multiplier=1)
        g.memset(ident1[:], 1.0)
        g.memset(ones_col[:], 1.0)
        g.memset(ones_row[:], 1.0)
        g.memset(ones_rf[:], 1.0)
        g.memset(epsv[:], EPS)
        return g.memset(tiny[:], 0.0)
    init_tok = P.em(pool, _init, waits=[("pool", init0_tok)], inc="pool")

    pps = contextlib.ExitStack()
    ptr = [pps.enter_context(nc.psum_tensor(f"ptr{i}", [128, 1024], BF16))
           for i in range(2)]
    vf32 = varena.bitcast(F32)
    sstg = qarena
    cast_tok, tr_tok, cp_tok, stq_tok, ld_tok = {}, {}, {}, {}, {}
    sub_i = [0]

    def prep_chunk(ch, src_ap, nsub, dst_ap, stg_view, scale, in_i32):
        slot = ch % 4
        lsrc = vf32[:, slot * 4096: slot * 4096 + nsub * 1024]
        if in_i32:
            lsrc = lsrc.bitcast(I32)
        lsem = ("wla", "wlb", "qa", "qb")[slot]
        lt = (lsem, P.em(syn, lambda e, o=lsrc.rearrange(
                  "p (c d) -> p c d", c=nsub), i=src_ap:
                  e.dma_start(out=o, in_=i),
                  waits=[("act", cast_tok.get(ch - 4)), ("pool", init_tok)],
                  inc=lsem, amt=16))
        ld_tok[ch] = lt
        stg = sstg[:, (ch % 2) * 4096: (ch % 2) * 4096 + nsub * 1024]
        ct = pt = None
        for si in range(nsub):
            gsub = sub_i[0]
            sub_i[0] += 1
            lsl = gsub % 2
            csrc = lnout[:, lsl * 1024:(lsl + 1) * 1024]
            ct = P.em(act, lambda e, o=csrc,
                      i=lsrc[:, si * 1024:(si + 1) * 1024], sc=scale:
                      e.activation(out=o, in_=i, func=AF.Copy, scale=sc),
                      waits=[lt, ("pe", tr_tok.get(gsub - 2))], inc="act")
            tt = None
            for c in range(8):
                tt = P.em(pe, lambda e, o=ptr[lsl][:, c * 128:(c + 1) * 128],
                          i=csrc[:, c * 128:(c + 1) * 128]:
                          e.transpose(out=o, in_=i, identity=ident[:]),
                          waits=[("act", ct), ("dve", cp_tok.get(gsub - 2)),
                                 ("pool", init_tok)],
                          inc="pe" if c == 7 else None)
            tr_tok[gsub] = tt
            pt = P.em(dve, lambda e, o=stg[:, si * 1024:(si + 1) * 1024],
                      i=ptr[lsl][:, :]: e.tensor_copy(out=o, in_=i),
                      waits=[("pe", tt), stq_tok.get(ch - 2)], inc="dve")
            cp_tok[gsub] = pt
        cast_tok[ch] = ct
        psem = ("psa", "psb")[ch % 2]
        stq_tok[ch] = (psem, P.em(
            pool, lambda e, o=dst_ap, i=stg_view(stg):
            e.dma_start(out=o, in_=i),
            waits=[("dve", pt)], inc=psem, amt=16))

    ch = 0
    for wsrc, wdst, i32 in ((wqkv, wqkvT, False), (wg_i, wgT, True),
                            (wu_i, wuT, True)):
        for q in range(6):
            dst = wdst[4 * q: 4 * q + 4].rearrange("q c d k -> d q c k")
            prep_chunk(ch, wsrc[q * 512:(q + 1) * 512, :].rearrange(
                "(c p) d -> p c d", p=128), 4, dst,
                lambda st: st[:, 0:4096].rearrange("d (q2 c k) -> d q2 c k",
                                                   q2=4, c=8),
                1.0 if not i32 else 1 / 127.0, i32)
            ch += 1
    for dt_i in range(8):
        dst = wdT[0:24, :, dt_i * 128:(dt_i + 1) * 128].rearrange(
            "c d k -> d c k")
        prep_chunk(ch, wd_i[dt_i * 128:(dt_i + 1) * 128, :].rearrange(
            "p (c d) -> p c d", c=3), 3, dst,
            lambda st: st[:, 0:3072].rearrange("d (c k) -> d c k", c=24),
            1 / 127.0, True)
        ch += 1

    pbv = [pps.enter_context(nc.psum_tensor(f"pbv{i}", [128, 512], F32))
           for i in range(2)]
    bstg = parena.bitcast(F32)[0:16, 0:128]
    bstg2 = parena.bitcast(F32)[0:1, 1024:2048]
    b16 = lnout[0:16, 0:128]
    b16v = lnout[0:1, 1024:2048]
    bl = ("misc", P.em(syn, lambda e: e.dma_start(
        out=bstg, in_=bqkv[0:2048].rearrange("(c p) -> c p", p=128)),
        inc="misc", amt=16))
    blv = ("misc", P.em(syn, lambda e: e.dma_start(
        out=bstg2, in_=bqkv[2048:3072].rearrange("(a e) -> a e", a=1)),
        inc="misc", amt=16))
    c1 = P.em(act, lambda e: e.activation(out=b16, in_=bstg, func=AF.Copy),
              waits=[blv, ("pe", tr_tok[sub_i[0] - 2]), ("pool", init_tok)],
              inc="act")
    c2 = P.em(act, lambda e: e.activation(out=b16v, in_=bstg2, func=AF.Copy),
              waits=[blv, ("pe", tr_tok[sub_i[0] - 1])], inc="act")
    bt = P.em(pe, lambda e: e.transpose(out=ptr[0][:, 0:16], in_=b16,
                                        identity=ident[0:16, 0:16]),
              waits=[("act", c1), ("dve", cp_tok[sub_i[0] - 2])], inc="pe")
    bm0 = P.em(pe, lambda e: e.matmul(out=pbv[0][:, :], lhsT=ones_row[0:1, 0:128],
                                      rhs=b16v[:, 0:512], start=True, stop=True),
               waits=[("act", c2)], inc="pe")
    bm1 = P.em(pe, lambda e: e.matmul(out=pbv[1][:, :], lhsT=ones_row[0:1, 0:128],
                                      rhs=b16v[:, 512:1024], start=True, stop=True),
               inc="pe")
    bq_tok = P.em(act, lambda e: e.activation(out=bqT[:], in_=ptr[0][:, 0:16].bitcast(F32)[:, 0:16] if False else ptr[0][:, 0:16], func=AF.Copy),
                  waits=[("pe", bt)], inc="act")
    P.em(act, lambda e: e.activation(out=bvrep[:, 0:512], in_=pbv[0][:, :],
                                     func=AF.Copy), waits=[("pe", bm1)])
    bq_tok = P.em(act, lambda e: e.activation(out=bvrep[:, 512:1024],
                                              in_=pbv[1][:, :], func=AF.Copy),
                  inc="act")

    pps.close()
    snap_full = [(sn, P.cnt[sn]) for sn in P.FENCE_SEMS]
    snap_syn = [(sn, P.cnt[sn]) for sn in ("psa", "psb", "act", "pool")]
    for e_ in ("tensor", "scalar", "vector", "gpsimd"):
        P.em(e_, None, waits=snap_full)
    P.em(syn, None, waits=snap_syn)

    ring_last = {}

    for s in range(NSEQ):
        bq_gate = None
        p1s = contextlib.ExitStack()
        trP = [p1s.enter_context(nc.psum_tensor(f"s{s}trP{i}", [128, 1024], BF16))
               for i in range(2)]
        pqk = [p1s.enter_context(nc.psum_tensor(f"s{s}pqk{i}", [128, 512], F32))
               for i in range(2)]
        pv = [p1s.enter_context(nc.psum_tensor(f"s{s}pv{i}", [128, 512], F32))
              for i in range(2)]
        pqn = p1s.enter_context(nc.psum_tensor(f"s{s}pqn", [1, 512], F32))
        pkn = p1s.enter_context(nc.psum_tensor(f"s{s}pkn", [1, 512], F32))

        LNxT = parena[:, 0: 8 * 512]
        QTst = parena[:, 8 * 512: 16 * 512]
        Kst = parena[:, 16 * 512: 24 * 512]

        apply_tok, transp_tok, copy_tok = {}, {}, {}
        dbg_gate = {}
        ev_tok, vql_tok = {}, {}
        wset_tok = {}
        qn_tok, kn_tok, qmax_tok, kmax_tok = {}, {}, {}, {}
        vlast_tok = {}
        qt_dma, kscr_dma = {}, {}
        qt_relay, ks_relay = {}, {}
        sq2_tok = {}

        def wload(g, si):
            gs = g * 6 + si
            par = gs % 2
            dst = wbufs[:, par * 4096:(par + 1) * 4096].rearrange(
                "d (ee c k) -> d ee c k", ee=4, c=8)
            src = wqkvT[4 * si: 4 * si + 4].rearrange("ee c d k -> d ee c k")
            if si < 2:
                free = ("pe", vlast_tok.get(g - 1))
            else:
                free = ("pe", P.cnt["pe"])
            wsem = "wla" if par == 0 else "wlb"
            wset_tok[gs] = (wsem, P.em(syn, lambda e, o=dst, i=src:
                                e.dma_start(out=o, in_=i),
                                waits=[free], inc=wsem, amt=16))

        def chain_p1_load(tile):
            sl4 = tile % 4
            xsrc = xa[:, sl4 * 1024:(sl4 + 1) * 1024]
            lsem = ("lda", "ldb", "ldc", "ldd")[sl4]
            load_tok[tile] = (lsem, P.em(
                syn, lambda e, o=xsrc, i=t_in[s, tile * 128:(tile + 1) * 128, :]:
                e.dma_start(out=o, in_=i),
                waits=[apply_tok.get(tile - 4), dbg_gate.get(tile - 2)],
                inc=lsem, amt=16))

        def chain_p1_ln(tile):
            sl4 = tile % 4
            r = tile % 2
            xsrc = xa[:, sl4 * 1024:(sl4 + 1) * 1024]
            lo = lnout[:, sl4 * 1024:(sl4 + 1) * 1024]
            rl = ring_last.get(r)
            extra = [load_tok[tile]] + ([("act", rl[0]), ("dve", rl[1])]
                                        if rl else [])
            ag, rs = ln_stats(r, xsrc, extra)
            ap_ = P.em(dve, lambda e, o=lo, i=xsrc, m=st_mean(r), rr=st_rstd(r):
                       e.tensor_scalar(out=o, in0=i, scalar1=m, scalar2=rr,
                                       op0=ALU.subtract, op1=ALU.mult),
                       waits=[("act", rs), ("dve", ag),
                              ("pe", transp_tok.get(tile - 4))],
                       inc="dve")
            apply_tok[tile] = ("dve", ap_)
            ring_last[r] = (rs, ap_)

        load_tok = {}
        for tile in range(4):
            chain_p1_load(tile)
        for tile in range(4):
            chain_p1_ln(tile)

        for g in range(NG1):
            wload(g, 0)
            wload(g, 1)

            for j in range(4):
                tile = 4 * g + j
                sl = tile % 2
                sl4 = tile % 4
                lo = lnout[:, sl4 * 1024:(sl4 + 1) * 1024]
                ap_ = apply_tok[tile]
                tt = None
                for c in range(8):
                    tt = P.em(pe, lambda e, o=trP[sl][:, c * 128:(c + 1) * 128],
                              i=lo[:, c * 128:(c + 1) * 128]:
                              e.transpose(out=o, in_=i, identity=ident[:]),
                              waits=[ap_, ("dve", copy_tok.get(tile - 2))],
                              inc="pe" if c == 7 else None)
                transp_tok[tile] = tt
                dstv = LNxT.rearrange("d (c tk) -> d c tk", c=8)[
                    :, :, j * 128:(j + 1) * 128]
                srcv = trP[sl][:, :].rearrange("d (c tk) -> d c tk", c=8)
                copy_tok[tile] = P.em(
                    dve, lambda e, o=dstv, i=srcv: e.tensor_copy(out=o, in_=i),
                    waits=[("pe", tt), ("pe", vlast_tok.get(g - 1))], inc="dve")
            lnx_ready = copy_tok[4 * g + 3]

            for e_ in range(16):
                gs = g * 6 + e_ // 4
                par = gs % 2
                wv = wbufs[:, par * 4096:(par + 1) * 4096]
                bank = e_ % 2
                mwaits = [("dve", lnx_ready), wset_tok[gs],
                          ("act", ev_tok.get((g, e_ - 2),
                                             ev_tok.get((g - 1, e_ + 14)))),
                          ("act", bq_tok if g == 0 else None)]
                mmt = None
                for c in range(8):
                    lhs = wv[:, (e_ % 4) * 1024 + c * 128:
                             (e_ % 4) * 1024 + (c + 1) * 128]
                    rhs = LNxT[:, c * 512:(c + 1) * 512]
                    mmt = P.em(pe, lambda e, o=pqk[bank][:, :], l=lhs, rr=rhs,
                               st=(c == 0), sp=(c == 7):
                               e.matmul(out=o, lhsT=l, rhs=rr, start=st, stop=sp),
                               waits=mwaits if c == 0 else (),
                               inc="pe" if c == 7 else None)
                if e_ < 8:
                    dst = QTst[:, e_ * 512:(e_ + 1) * 512]
                    stage_free = qt_dma.get(g - 1)
                else:
                    dst = Kst[:, (e_ - 8) * 512:(e_ - 8 + 1) * 512]
                    stage_free = kscr_dma.get(g - 1)
                et = P.em(act, lambda e, o=dst, i=pqk[bank][:, :],
                          b=bqT[:, e_: e_ + 1]:
                          e.activation(out=o, in_=i, func=AF.Identity, bias=b),
                          waits=[("pe", mmt), stage_free], inc="act")
                ev_tok[(g, e_)] = et
                slq = e_ % 8
                sq_w = [("act", et)]
                if e_ < 8:
                    sq_w.append(("pe", kn_tok.get(g - 1)))
                else:
                    sq_w.append(("pe", qn_tok.get(g)))
                sq2_tok[(g, e_)] = P.em(
                    dve, lambda e, o=q2s[:, slq * 512:(slq + 1) * 512], i=dst:
                    e.tensor_mul(out=o, in0=i, in1=i), waits=sq_w, inc="dve")

                if e_ % 4 == 3:
                    wload(g, e_ // 4 + 2)
                if e_ == 3 and g + 1 < NG1:
                    for tl in range(4 * (g + 1), 4 * (g + 1) + 4):
                        chain_p1_load(tl)
                if e_ in (8, 10, 12, 14) and g + 1 < NG1:
                    chain_p1_ln(4 * (g + 1) + (e_ - 8) // 2)
                if e_ == 7:
                    tok = None
                    for c in range(8):
                        tok = P.em(pe, lambda e, o=pqn[:, :],
                                   rr=q2s[:, c * 512:(c + 1) * 512],
                                   st=(c == 0), sp=(c == 7):
                                   e.matmul(out=o, lhsT=ones_col[:], rhs=rr,
                                            start=st, stop=sp),
                                   waits=[("dve", sq2_tok[(g, 7)]),
                                          ("dve", qmax_tok.get(g - 1))],
                                   inc="pe" if c == 7 else None)
                    qn_tok[g] = tok
                    rm = P.em(dve, lambda e, i=pqn[:, :]: e.reduce_max(
                        out=tiny[:, 5:6], in_=i, axis=AX.X),
                        waits=[("pe", tok)], inc="dve")
                    qmax_tok[g] = P.em(dve, lambda e: e.tensor_max(
                        out=tiny[:, 0:1], in0=tiny[:, 0:1], in1=tiny[:, 5:6]),
                        waits=[("dve", rm)], inc="dve")

            for j in range(4):
                tile = 4 * g + j
                for hf in range(2):
                    gs = g * 6 + 4 + hf
                    par = gs % 2
                    wv4 = wbufs[:, par * 4096:(par + 1) * 4096].rearrange(
                        "d (ee c k) -> d ee c k", ee=4, c=8)
                    mwaits = [("dve", lnx_ready), wset_tok[gs],
                              ("dve", vql_tok.get((g, j - 1, hf),
                                                  vql_tok.get((g - 1, 3, hf))))]
                    mmt = None
                    for c in range(8):
                        lhs = LNxT[:, c * 512 + j * 128: c * 512 + (j + 1) * 128]
                        mmt = P.em(pe, lambda e, o=pv[hf][:, :], l=lhs,
                                   rr=wv4[:, :, c, :], st=(c == 0), sp=(c == 7):
                                   e.matmul(out=o, lhsT=l, rhs=rr,
                                            start=st, stop=sp),
                                   waits=mwaits if c == 0 else (),
                                   inc="pe" if c == 7 else None)
                    dst = varena[:, tile * 1024 + hf * 512:
                                 tile * 1024 + (hf + 1) * 512]
                    vql_tok[(g, j, hf)] = P.em(
                        dve, lambda e, o=dst, i=pv[hf][:, :],
                        b=bvrep[:, hf * 512:(hf + 1) * 512]:
                        e.tensor_add(out=o, in0=i, in1=b),
                        waits=[("pe", mmt)], inc="dve")
                if j == 0:
                        tok = None
                        for c in range(8):
                            tok = P.em(pe, lambda e, o=pkn[:, :],
                                       rr=q2s[:, c * 512:(c + 1) * 512],
                                       st=(c == 0), sp=(c == 7):
                                       e.matmul(out=o, lhsT=ones_col[:], rhs=rr,
                                                start=st, stop=sp),
                                       waits=[("dve", sq2_tok[(g, 15)]),
                                              ("dve", kmax_tok.get(g - 1))],
                                       inc="pe" if c == 7 else None)
                        kn_tok[g] = tok
                        rm = P.em(dve, lambda e, i=pkn[:, :]: e.reduce_max(
                            out=tiny[:, 6:7], in_=i, axis=AX.X),
                            waits=[("pe", tok)], inc="dve")
                        kmax_tok[g] = P.em(dve, lambda e: e.tensor_max(
                            out=tiny[:, 1:2], in0=tiny[:, 1:2], in1=tiny[:, 6:7]),
                            waits=[("dve", rm)], inc="dve")

            vlast_tok[g] = P.cnt["pe"]

            qt_dma[g] = ("qt", P.em(pool, lambda e,
                             o=qscr[s][g].rearrange("c d q -> d c q"),
                             i=QTst.rearrange("d (c q) -> d c q", c=8):
                             e.dma_start(out=o, in_=i),
                             waits=[("act", ev_tok[(g, 7)])], inc="qt", amt=16))
            qt_relay[g] = P.em(pool, lambda e: e.nop(),
                               waits=[("qt", P.cnt["qt"])], inc="qtr")
            tok = None
            for ec in range(8):
                ksrc = Kst[:, ec * 512:(ec + 1) * 512].rearrange(
                    "d (tt k) -> d tt k", tt=4)
                kdst = kscr[s][4 * g: 4 * g + 4, ec].rearrange("tt d k -> d tt k")
                tok = P.em(pool, lambda e, o=kdst, i=ksrc:
                           e.dma_start(out=o, in_=i),
                           waits=[("act", ev_tok[(g, 15)])], inc="ks", amt=16)
            kscr_dma[g] = ("ks", tok)
            ks_relay[g] = P.em(pool, lambda e: e.nop(),
                               waits=[("ks", P.cnt["ks"])], inc="ksr")

        P.em(act, lambda e: e.activation(out=tiny[:, 2:3], in_=tiny[:, 0:1],
                                         func=AF.Ln),
             waits=[("dve", qmax_tok[NG1 - 1]), ("dve", kmax_tok[NG1 - 1])])
        l2 = P.em(act, lambda e: e.activation(out=tiny[:, 3:4], in_=tiny[:, 1:2],
                                              func=AF.Ln), inc="act")
        ad = P.em(dve, lambda e: e.tensor_add(out=tiny[:, 2:3], in0=tiny[:, 2:3],
                                              in1=tiny[:, 3:4]),
                  waits=[("act", l2)], inc="dve")
        bb = P.em(act, lambda e: e.activation(out=tiny[:, 4:5], in_=tiny[:, 2:3],
                                              func=AF.Exp, scale=0.5),
                  waits=[("dve", ad)], inc="act")
        nb_ = P.em(dve, lambda e: e.tensor_scalar(
            out=tiny[:, 4:5], in0=tiny[:, 4:5], scalar1=-ISQ, scalar2=None,
            op0=ALU.mult), waits=[("act", bb)], inc="dve")
        bm = P.em(pe, lambda e, o=pv[0][:, 0:1]: e.matmul(
            out=o, lhsT=ones_rf[:], rhs=tiny[:, 4:5], start=True, stop=True),
                  waits=[("dve", nb_), ("dve", vql_tok[(NG1 - 1, 3, 0)])],
                  inc="pe")
        be = P.em(act, lambda e, i=pv[0][:, 0:1]: e.activation(
            out=bias_exp[:], in_=i, func=AF.Copy),
                  waits=[("pe", bm)], inc="act")

        if dbg and s == 0:
            P.em(pool, lambda e: e.dma_start(out=d_tiny, in_=tiny[:]),
                 waits=[("act", be)], inc="misc", amt=16)
            P.em(pool, lambda e: e.dma_start(out=d_lnxt, in_=parena[:, 0:4096]),
                 waits=[("pe", vlast_tok[NG1 - 1])], inc="misc", amt=16)

        p1s.close()

        p2s = contextlib.ExitStack()
        psc = [p2s.enter_context(nc.psum_tensor(f"s{s}psc{i}", [128, 512], F32))
               for i in range(3)]
        pav = [p2s.enter_context(nc.psum_tensor(f"s{s}pav{i}", [128, 1024], F32))
               for i in range(2)]
        psb7 = p2s.enter_context(nc.psum_tensor(f"s{s}psb7", [128, 512], F32))

        exp_tok, sc_tok = {}, {}
        rc_last = {}
        probs_free = {}
        q_tok, q_free = {}, {}
        kset_tok, kset_free = {}, {}
        evac_tok, add_tok, xn_dma = {}, {}, {}
        scale_cp_tok = {}

        def emit_av_phase(B):
            stok = None
            for ti in range(32):
                stok = P.em(pe, lambda e, o=psb7[0:1, :],
                            rr=parena[:, ti * 512:(ti + 1) * 512],
                            st=(ti == 0), sp=(ti == 31):
                            e.matmul(out=o, lhsT=ones_col[:], rhs=rr,
                                     start=st, stop=sp),
                            waits=([("act", exp_tok[(B, 31)]),
                                    ("act", scale_cp_tok.get(B - 1))]
                                   if ti == 0 else ()),
                            inc="pe" if ti == 31 else None)
            rc = P.em(dve, lambda e, i=psb7[0:1, :]: e.reciprocal(
                out=recip_row[:], in_=i),
                      waits=[("pe", stok)], inc="dve")
            rc_last["rc"] = rc
            ttok = None
            for j in range(4):
                ttok = P.em(pe, lambda e, o=psb7[:, 4 + j: 5 + j],
                            i=recip_row[0:1, j * 128:(j + 1) * 128]:
                            e.transpose(out=o, in_=i, identity=ident1[:]),
                            waits=[("dve", rc)], inc="pe" if j == 3 else None)
            sct = None
            for j in range(4):
                sct = P.em(act, lambda e, o=st_scale(j), i=psb7[:, 4 + j: 5 + j]:
                           e.activation(out=o, in_=i, func=AF.Copy),
                           waits=[("pe", ttok)], inc="act")
            scale_cp_tok[B] = sct
            for j in range(4):
                pr = j % 2
                tile = 4 * B + j
                r = j % 2
                w0 = [("act", exp_tok[(B, 31)]),
                      ("act", evac_tok.get((B, j - 2),
                                           evac_tok.get((B - 1, 2 + (j % 2)))))]
                av1 = None
                for hf in range(2):
                    for ti in range(32):
                        lhs = parena[:, ti * 512 + j * 128:
                                     ti * 512 + (j + 1) * 128]
                        rhs = varena[:, ti * 1024 + hf * 512:
                                     ti * 1024 + (hf + 1) * 512]
                        av1 = P.em(pe, lambda e,
                                   o=pav[pr][:, hf * 512:(hf + 1) * 512],
                                   l=lhs, rr=rhs, st=(ti == 0), sp=(ti == 31):
                                   e.matmul(out=o, lhsT=l, rhs=rr,
                                            start=st, stop=sp),
                                   waits=w0 if (ti == 0 and hf == 0) else (),
                                   inc="pe" if ti == 31 else None)
                xsl = xb[:, (j % 2) * 1024:((j % 2) + 1) * 1024]
                xsem = "xba" if j % 2 == 0 else "xbb"
                xl = (xsem, P.em(syn, lambda e, o=xsl,
                          i=t_in[s, tile * 128:(tile + 1) * 128, :]:
                          e.dma_start(out=o, in_=i),
                          waits=[("dve", add_tok.get((B, j - 2),
                                                     add_tok.get((B - 1, 2 + (j % 2)))))],
                          inc=xsem, amt=16))
                esl = evac[:, (j % 2) * 1024:((j % 2) + 1) * 1024]
                et = P.em(act, lambda e, o=esl, i=pav[pr][:, :], sc=st_scale(j):
                          e.activation(out=o, in_=i, func=AF.Copy, scale=sc),
                          waits=[("pe", av1), ("act", sct),
                                 xn_dma.get((B, j - 2),
                                            xn_dma.get((B - 1, 2 + (j % 2))))],
                          inc="act")
                evac_tok[(B, j)] = et
                rl = ring_last.get(r)
                extra = [("act", et)] + ([("act", rl[0]), ("dve", rl[1])]
                                         if rl else [])
                ag, rs = ln_stats(r, esl, extra)
                ts = P.em(dve, lambda e, o=dn2[:], i=esl, m=st_mean(r),
                          rr=st_rstd(r):
                          e.tensor_scalar(out=o, in0=i, scalar1=m, scalar2=rr,
                                          op0=ALU.subtract, op1=ALU.mult),
                          waits=[("act", rs), ("dve", ag)], inc="dve")
                ring_last[r] = (rs, ts)
                at = P.em(dve, lambda e, o=esl, i=dn2[:], x=xsl:
                          e.tensor_add(out=o, in0=i, in1=x),
                          waits=[("dve", ts), xl], inc="dve")
                add_tok[(B, j)] = at
                ssem = "sxa" if j % 2 == 0 else "sxb"
                xn_dma[(B, j)] = (ssem, P.em(pool, lambda e, o=xnew[s][tile], i=esl:
                                      e.dma_start(out=o, in_=i),
                                      waits=[("dve", at)], inc=ssem, amt=16))
                if j == 3:
                    probs_free[B] = av1

        for b in range(NB):
            qdst = qarena[:, (b % 2) * 4096:((b % 2) + 1) * 4096].rearrange(
                "d (c q) -> d c q", c=8)
            qsem = "qa" if b % 2 == 0 else "qb"
            q_tok[b] = (qsem, P.em(syn, lambda e, o=qdst,
                            i=qscr[s][b].rearrange("c d q -> d c q"):
                            e.dma_start(out=o, in_=i),
                            waits=[("pe", q_free.get(b - 2)),
                                   ("qtr", qt_relay[b])],
                            inc=qsem, amt=16))
            if b > 0:
                emit_av_phase(b - 1)
            for ti in range(32):
                if ti % 4 == 0:
                    si = ti // 4
                    kdst = kbufs[:, (si % 2) * 4096:((si % 2) + 1) * 4096].rearrange(
                        "d (tt c k) -> d tt c k", tt=4, c=8)
                    ksrc = kscr[s][4 * si: 4 * si + 4].rearrange(
                        "tt c d k -> d tt c k")
                    ksem = "ka" if si % 2 == 0 else "kb"
                    kset_tok[(b, si)] = (ksem, P.em(
                        syn, lambda e, o=kdst, i=ksrc: e.dma_start(out=o, in_=i),
                        waits=[("pe", kset_free.get((b, si - 2),
                                                    kset_free.get((b - 1, 6 + si)))),
                               ("ksr", ks_relay[si])],
                        inc=ksem, amt=16))
                bank = ti % 3
                if ti >= 3:
                    prev_exp = exp_tok[(b, ti - 3)]
                else:
                    prev_exp = exp_tok.get((b - 1, {0: 30, 1: 31, 2: 29}[ti]))
                w0 = [kset_tok[(b, ti // 4)], q_tok[b],
                      ("act", prev_exp), ("act", be),
                      ("qtr", qt_relay[NG1 - 1]), ("ksr", ks_relay[NG1 - 1])]
                mm = None
                for c in range(8):
                    lhs = kbufs[:, (((ti // 4) % 2) * 4 + ti % 4) * 1024 + c * 128:
                                (((ti // 4) % 2) * 4 + ti % 4) * 1024 + (c + 1) * 128]
                    rhs = qarena[:, (b % 2) * 4096 + c * 512:
                                 (b % 2) * 4096 + (c + 1) * 512]
                    mm = P.em(pe, lambda e, o=psc[bank][:, :], l=lhs, rr=rhs,
                              st=(c == 0), sp=(c == 7):
                              e.matmul(out=o, lhsT=l, rhs=rr, start=st, stop=sp),
                              waits=w0 if c == 0 else (),
                              inc="pe" if c == 7 else None)
                sc_tok[(b, ti)] = mm
                if ti % 4 == 3:
                    kset_free[(b, ti // 4)] = mm
                exp_tok[(b, ti)] = P.em(
                    act, lambda e, o=parena[:, ti * 512:(ti + 1) * 512],
                    i=psc[bank][:, :]:
                    e.activation(out=o, in_=i, func=AF.Exp, scale=ISQ,
                                 bias=bias_exp[:]),
                    waits=[("pe", mm)], inc="act")
            q_free[b] = sc_tok[(b, 31)]
        emit_av_phase(NB - 1)

        if dbg and s == 0:
            P.em(pool, lambda e: e.dma_start(out=d_probs, in_=parena[:, :]),
                 waits=[("act", exp_tok[(NB - 1, 31)])], inc="misc", amt=16)
            P.em(pool, lambda e: e.dma_start(out=d_recip, in_=recip_row[:]),
                 waits=[("dve", rc_last["rc"])], inc="misc", amt=16)
            P.em(pool, lambda e: e.dma_start(out=d_scale, in_=stats[:, 48:52]),
                 waits=[("act", scale_cp_tok[NB - 1])], inc="misc", amt=16)

        p2s.close()
        p2_act_done = evac_tok[(NB - 1, 3)]
        p2_av_done = probs_free[NB - 1]

        p3s = contextlib.ExitStack()
        pg = p3s.enter_context(nc.psum_tensor(f"s{s}pg", [128, 1024], F32))
        pu = p3s.enter_context(nc.psum_tensor(f"s{s}pu", [128, 1024], F32))
        pd = [p3s.enter_context(nc.psum_tensor(f"s{s}pd{i}", [128, 1024], F32))
              for i in range(2)]
        trv = pg[:, :].bitcast(BF16)
        LNx2T = varena[:, 0: 8 * 1024]

        def hact_ap(h):
            if h < 16:
                return parena[:, h * 1024:(h + 1) * 1024]
            return qarena[:, (h - 16) * 1024:(h - 16 + 1) * 1024]

        apply3_tok, transp3_tok, copy3_tok = {}, {}, {}
        silu_tok, hact_tok, gate_tok, up_tok = {}, {}, {}, {}
        wpair_tok, wpair_free = {}, {}
        dn_ev_tok, add3_tok, out_dma = {}, {}, {}
        gu_last, dn_last = {}, {}

        load3_tok = {}

        def chain_p3_load(g, j):
            tile = 8 * g + j
            sl4 = j % 4
            xsrc = xa[:, sl4 * 1024:(sl4 + 1) * 1024]
            lsem = ("lda", "ldb", "ldc", "ldd")[sl4]
            xw = [apply3_tok.get((g, j - 4), apply3_tok.get((g - 1, j + 4))),
                  xn_dma[(tile // 4, tile % 4)]]
            if g == 0 and j < 4:
                xw.append(apply_tok.get(28 + j))
            load3_tok[(g, j)] = (lsem, P.em(
                syn, lambda e, o=xsrc, i=xnew[s][tile]:
                e.dma_start(out=o, in_=i), waits=xw, inc=lsem, amt=16))

        def chain_p3_ln(g, j):
            sl4 = j % 4
            r = j % 2
            xsrc = xa[:, sl4 * 1024:(sl4 + 1) * 1024]
            lo = lnout[:, sl4 * 1024:(sl4 + 1) * 1024]
            rl = ring_last.get(r)
            extra = [load3_tok[(g, j)]] + ([("act", rl[0]), ("dve", rl[1])]
                                           if rl else [])
            ag, rs = ln_stats(r, xsrc, extra)
            aw = [("act", rs), ("dve", ag),
                  ("pe", transp3_tok.get((g, j - 4),
                                         transp3_tok.get((g - 1, j + 4))))]
            if g == 0 and j < 4:
                aw.append(("pe", transp_tok[28 + j]))
            ap_ = P.em(dve, lambda e, o=lo, i=xsrc, m=st_mean(r), rr=st_rstd(r):
                       e.tensor_scalar(out=o, in0=i, scalar1=m, scalar2=rr,
                                       op0=ALU.subtract, op1=ALU.mult),
                       waits=aw, inc="dve")
            apply3_tok[(g, j)] = ("dve", ap_)
            ring_last[r] = (rs, ap_)

        for jj in range(4):
            chain_p3_load(0, jj)
        wd_tok = None
        for q4 in range(6):
            ddst = varena[:, (8 + 4 * q4) * 1024:(8 + 4 * q4 + 4) * 1024].rearrange(
                "d (c k) -> d c k", c=4)
            wd_tok = ("misc", P.em(syn, lambda e, o=ddst,
                          i=wdT[4 * q4: 4 * q4 + 4].rearrange("c d k -> d c k"):
                          e.dma_start(out=o, in_=i),
                          waits=[("pe", p2_av_done)], inc="misc", amt=16))


        def p3_transp(g, j):
                sl4 = j % 4
                lo = lnout[:, sl4 * 1024:(sl4 + 1) * 1024]
                ap_ = apply3_tok[(g, j)]
                sl = j % 2
                tt = None
                for c in range(8):
                    tt = P.em(pe, lambda e,
                              o=trv[:, sl * 1024 + c * 128: sl * 1024 + (c + 1) * 128],
                              i=lo[:, c * 128:(c + 1) * 128]:
                              e.transpose(out=o, in_=i, identity=ident[:]),
                              waits=[ap_, ("dve", copy3_tok.get((g, j - 2))),
                                     ("pe", gu_last.get(g - 1)),
                                     ("act", silu_tok.get((g - 1, 23))),
                                     ("act", exp_tok[(NB - 1, 31)]
                                      if g == 0 else None)],
                              inc="pe" if c == 7 else None)
                transp3_tok[(g, j)] = tt
                dstv = LNx2T.rearrange("d (c tk) -> d c tk", c=8)[
                    :, :, j * 128:(j + 1) * 128]
                srcv = trv[:, sl * 1024:(sl + 1) * 1024].rearrange(
                    "d (c tk) -> d c tk", c=8)
                copy3_tok[(g, j)] = P.em(
                    dve, lambda e, o=dstv, i=srcv: e.tensor_copy(out=o, in_=i),
                    waits=[("pe", tt), ("pe", gu_last.get(g - 1))], inc="dve")

        for g in range(NG3):
            for j in range(4):
                if (g, j) not in load3_tok:
                    chain_p3_load(g, j)
            for j in range(4):
                if (g, j) not in apply3_tok:
                    chain_p3_ln(g, j)
            for j in range(4, 8):
                if (g, j) not in load3_tok:
                    chain_p3_load(g, j)
            for j in range(4):
                p3_transp(g, j)
            for j in range(4, 8):
                chain_p3_ln(g, j)
            for j in range(4, 8):
                p3_transp(g, j)
            lnx2_ready = copy3_tok[(g, 7)]

            for h in range(24):
                pi = h // 2
                par = pi % 2
                hh = h % 2
                if hh == 0:
                    wsem = "wla" if par == 0 else "wlb"
                    for wi, (wt, off) in enumerate([(wgT, 0), (wuT, 2048)]):
                        ddst = wbufs[:, par * 4096 + off: par * 4096 + off + 2048]
                        ddst = ddst.rearrange("d (hc c k) -> d hc c k", hc=2, c=8)
                        tok = P.em(
                            syn, lambda e, o=ddst,
                            i=wt[h:h + 2].rearrange("hc c d k -> d hc c k"):
                            e.dma_start(out=o, in_=i),
                            waits=[("pe", wpair_free.get((g, pi - 2),
                                                         wpair_free.get((g - 1, 10 + pi)))),
                                   ("pe", vlast_tok[NG1 - 1] if g == 0 else None)],
                            inc=wsem, amt=16)
                    wpair_tok[(g, pi)] = (wsem, tok)
                gw = [("dve", lnx2_ready), wpair_tok[(g, pi)],
                      ("act", silu_tok.get((g, h - 1), silu_tok.get((g - 1, 23))))]
                mm = None
                for hf in range(2):
                    for c in range(8):
                        lhs = wbufs[:, par * 4096 + hh * 1024 + c * 128:
                                    par * 4096 + hh * 1024 + (c + 1) * 128]
                        rhs = LNx2T[:, c * 1024 + hf * 512: c * 1024 + (hf + 1) * 512]
                        mm = P.em(pe, lambda e, o=pg[:, hf * 512:(hf + 1) * 512],
                                  l=lhs, rr=rhs, st=(c == 0), sp=(c == 7):
                                  e.matmul(out=o, lhsT=l, rhs=rr, start=st, stop=sp),
                                  waits=gw if (c == 0 and hf == 0) else (),
                                  inc="pe" if (c == 7 and hf == 1) else None)
                gate_tok[(g, h)] = mm
                uw = [wpair_tok[(g, pi)],
                      ("dve", hact_tok.get((g, h - 1), hact_tok.get((g - 1, 23)))),
                      ("act", p2_act_done if (g == 0 and h == 0) else None)]
                for hf in range(2):
                    for c in range(8):
                        lhs = wbufs[:, par * 4096 + 2048 + hh * 1024 + c * 128:
                                    par * 4096 + 2048 + hh * 1024 + (c + 1) * 128]
                        rhs = LNx2T[:, c * 1024 + hf * 512: c * 1024 + (hf + 1) * 512]
                        mm = P.em(pe, lambda e, o=pu[:, hf * 512:(hf + 1) * 512],
                                  l=lhs, rr=rhs, st=(c == 0), sp=(c == 7):
                                  e.matmul(out=o, lhsT=l, rhs=rr, start=st, stop=sp),
                                  waits=uw if (c == 0 and hf == 0) else (),
                                  inc="pe" if (c == 7 and hf == 1) else None)
                up_tok[(g, h)] = mm
                if hh == 1:
                    wpair_free[(g, pi)] = mm
                if h == 16 and g + 1 < NG3:
                    for jn in range(4):
                        chain_p3_load(g + 1, jn)
                if h == 23:
                    gu_last[g] = mm
                st_ = P.em(act, lambda e, o=silu_sb[:], i=pg[:, :]:
                           e.activation(out=o, in_=i, func=AF.Silu),
                           waits=[("pe", gate_tok[(g, h)]),
                                  ("dve", hact_tok.get((g, h - 1),
                                                       hact_tok.get((g - 1, 23))))],
                           inc="act")
                silu_tok[(g, h)] = st_
                hact_tok[(g, h)] = P.em(
                    dve, lambda e, o=hact_ap(h), i=silu_sb[:], u=pu[:, :]:
                    e.tensor_mul(out=o, in0=i, in1=u),
                    waits=[("act", st_), ("pe", up_tok[(g, h)]),
                           ("pe", dn_last.get(g - 1))],
                    inc="dve")
                if h >= 20 and g + 1 < NG3:
                    chain_p3_ln(g + 1, h - 20)
            hact_ready = hact_tok[(g, 23)]

            for j in range(8):
                tile = 8 * g + j
                pr = j % 2
                r = j % 2
                xsl = xb[:, (j % 2) * 1024:((j % 2) + 1) * 1024]
                xsem = "xba" if j % 2 == 0 else "xbb"
                xl = (xsem, P.em(syn, lambda e, o=xsl, i=xnew[s][tile]:
                          e.dma_start(out=o, in_=i),
                          waits=[("dve", add3_tok.get((g, j - 2),
                                                      add3_tok.get((g - 1, 6 + (j % 2))))),
                                 xn_dma[(tile // 4, tile % 4)]],
                          inc=xsem, amt=16))
                w0 = [("dve", hact_ready), wd_tok,
                      ("act", dn_ev_tok.get((g, j - 2),
                                            dn_ev_tok.get((g - 1, 6 + (j % 2))))),
                      ("act", p2_act_done if (g == 0 and j < 2) else None),
                      ("act", scale_cp_tok[NB - 1] if (g == 0 and j < 2) else None)]
                mm = None
                for hf in range(2):
                    for h in range(24):
                        lhs = hact_ap(h)[:, j * 128:(j + 1) * 128]
                        rhs = varena[:, (8 + h) * 1024 + hf * 512:
                                     (8 + h) * 1024 + (hf + 1) * 512]
                        mm = P.em(pe, lambda e,
                                  o=pd[pr][:, hf * 512:(hf + 1) * 512],
                                  l=lhs, rr=rhs, st=(h == 0), sp=(h == 23):
                                  e.matmul(out=o, lhsT=l, rhs=rr,
                                           start=st, stop=sp),
                                  waits=w0 if (h == 0 and hf == 0) else (),
                                  inc="pe" if (h == 23 and hf == 1) else None)
                if j == 7:
                    dn_last[g] = mm
                esl = evac[:, (j % 2) * 1024:((j % 2) + 1) * 1024]
                et = P.em(act, lambda e, o=esl, i=pd[pr][:, :]:
                          e.activation(out=o, in_=i, func=AF.Copy),
                          waits=[("pe", mm),
                                 out_dma.get((g, j - 2),
                                             out_dma.get((g - 1, 6 + (j % 2)))),
                                 xn_dma[(NB - 1, 2 + (j % 2))] if (g == 0 and j < 2)
                                 else None],
                          inc="act")
                dn_ev_tok[(g, j)] = et
                rl = ring_last.get(r)
                extra = [("act", et)] + ([("act", rl[0]), ("dve", rl[1])]
                                         if rl else [])
                ag, rs = ln_stats(r, esl, extra)
                ts = P.em(dve, lambda e, o=dn2[:], i=esl, m=st_mean(r),
                          rr=st_rstd(r):
                          e.tensor_scalar(out=o, in0=i, scalar1=m, scalar2=rr,
                                          op0=ALU.subtract, op1=ALU.mult),
                          waits=[("act", rs), ("dve", ag)], inc="dve")
                ring_last[r] = (rs, ts)
                at = P.em(dve, lambda e, o=esl, i=dn2[:], x=xsl:
                          e.tensor_add(out=o, in0=i, in1=x),
                          waits=[("dve", ts), xl], inc="dve")
                add3_tok[(g, j)] = at
                ssem = "sxa" if j % 2 == 0 else "sxb"
                out_dma[(g, j)] = (ssem, P.em(
                    pool, lambda e, o=out[s, tile * 128:(tile + 1) * 128, :],
                    i=esl: e.dma_start(out=o, in_=i),
                    waits=[("dve", at)], inc=ssem, amt=16))

        p3s.close()
        P.fence()

    store_sems = ("sxa", "sxb", "qt", "ks", "psa", "psb")
    fw = [(sn, P.cnt[sn]) for sn in store_sems]
    P.em(pool, None, waits=fw)
    P.em(syn, None, waits=fw)

    es.close()

    semstack = contextlib.ExitStack()
    SH = {sn: semstack.enter_context(nc.semaphore(sn)) for sn in SEMS}

    with nc.Block() as block:
        def runner(ename):
            def run(eng):
                for waits, fn, incs in P.ops[ename]:
                    for (sn, v) in waits:
                        eng.wait_ge(SH[sn], v)
                    if fn is None:
                        continue
                    ins = fn(eng)
                    for (sn, amt) in incs:
                        ins = ins.then_inc(SH[sn], amt)
            return run
        block.tensor(runner("tensor"))
        block.scalar(runner("scalar"))
        block.vector(runner("vector"))
        block.gpsimd(runner("gpsimd"))
        block.sync(runner("sync"))

    semstack.close()
    return nc


_NC_CACHE = None


def kernel(L0_attn_t, L0_attn_W_qkv, L0_attn_b_qkv,
           L0_ffn_w_gate, L0_ffn_w_up, L0_ffn_w_down):
    global _NC_CACHE
    if _NC_CACHE is None:
        _NC_CACHE = build_nc()
    nc = _NC_CACHE

    t = np.ascontiguousarray(L0_attn_t, dtype=np.float32)
    wq = np.ascontiguousarray(L0_attn_W_qkv, dtype=np.float32)
    bqv = np.ascontiguousarray(L0_attn_b_qkv, dtype=np.float32)
    wg = np.ascontiguousarray(L0_ffn_w_gate, dtype=np.int32)
    wu = np.ascontiguousarray(L0_ffn_w_up, dtype=np.int32)
    wd = np.ascontiguousarray(L0_ffn_w_down, dtype=np.int32)

    in_maps = [{"t": t[2 * c: 2 * c + 2], "wqkv": wq, "bqkv": bqv,
                "wg": wg, "wu": wu, "wd": wd} for c in range(8)]
    res = run_bass_kernel_spmd(nc, in_maps, core_ids=list(range(8)))
    return np.concatenate([r["out"] for r in res.results], axis=0)
